# version 03 (final)
# baseline (speedup 1.0000x reference)
import math
from collections import defaultdict
from contextlib import ExitStack

import numpy as np
import ml_dtypes

import concourse.bass as bass
import concourse.mybir as mybir
from concourse.bass_utils import run_bass_kernel_spmd

F32 = mybir.dt.float32
BF16 = mybir.dt.bfloat16
ALU = mybir.AluOpType
ACT = mybir.ActivationFunctionType
AX = mybir.AxisListType

D = 2048
T = 2048
NCH = D // 128
NTT = T // 512
EPS = 1e-5
ENGS = ("sy", "te", "ve", "sc", "gp")

U_CONST = np.array([
    [0.6102175712585449, 0.6966079473495483, 0.06568920612335205, 0.6029453277587891,
     0.4143662452697754, 0.24911487102508545, 0.5782934427261353, 0.24924850463867188,
     0.959638237953186, 0.8191142082214355, 0.3645491600036621, 0.059471845626831055,
     0.9161832332611084, 0.6182148456573486, 0.8282822370529175, 0.6976723670959473],
    [0.756269097328186, 0.3594318628311157, 0.4266716241836548, 0.477594256401062,
     0.08340179920196533, 0.40768611431121826, 0.6278458833694458, 0.3251793384552002,
     0.35155820846557617, 0.15171575546264648, 0.5589374303817749, 0.8866020441055298,
     0.3201930522918701, 0.12432277202606201, 0.6967982053756714, 0.8069779872894287]],
    dtype=np.float64)


class Ctx:
    def __init__(self, nc, stack):
        self.nc = nc
        self.stack = stack
        self.ops = {e: [] for e in ENGS}
        self.cnt = defaultdict(int)
        self.sem_names = set()
        self.waited = {e: defaultdict(int) for e in ENGS}
        self.inputs = {}
        self.input_arrays = {}
        self.scratch = {}
        self.outputs = {}
        self.sb = {}
        self.pt = [stack.enter_context(nc.psum_tensor(f"pb{i}", [128, 512], F32))
                   for i in range(8)]
        self.mm_tk = [None] * 8
        self.ev_tk = [None] * 8

    def input(self, name, arr_fn, shape, dtype=F32):
        if name not in self.inputs:
            ap = self.nc.dram_tensor(name, list(shape), dtype, kind="ExternalInput").ap()
            self.inputs[name] = ap
            self.input_arrays[name] = arr_fn
        return self.inputs[name]

    def dram(self, name, shape, dtype=F32):
        if name not in self.scratch:
            self.scratch[name] = self.nc.dram_tensor(name, list(shape), dtype,
                                                     kind="Internal").ap()
        return self.scratch[name]

    def output(self, name, shape, dtype=F32):
        if name not in self.outputs:
            self.outputs[name] = self.nc.dram_tensor(name, list(shape), dtype,
                                                     kind="ExternalOutput").ap()
        return self.outputs[name]

    def constf(self, v):
        key = float(v)
        if not hasattr(self, "_consts"):
            self._consts = {}
        if key not in self._consts:
            t = self.sbuf(f"cst{len(self._consts)}", [128, 1])
            tk = self.op("ve", lambda eng, SS, _t=t, _v=key: eng.memset(_t[:], _v))
            self._consts[key] = (t, tk)
        return self._consts[key]

    def sbuf(self, name, shape, dtype=F32):
        if name not in self.sb:
            self.sb[name] = self.stack.enter_context(
                self.nc.sbuf_tensor(name, list(shape), dtype))
        return self.sb[name]

    def tick(self, sem, n=1):
        self.sem_names.add(sem)
        self.cnt[sem] += n
        return (sem, self.cnt[sem])

    def need(self, e, waits):
        out = []
        for w in waits or []:
            if w is None:
                continue
            s, v = w
            if s.endswith("#hwm"):
                s = s[:-4]
                self.sem_names.add(s)
                if self.waited[e][s] < v:
                    self.waited[e][s] = v
                    out.append((s, v))
                continue
            if s.startswith("dm"):
                v = self.cnt[s]
                if not hasattr(self, "hwm"):
                    self.hwm = defaultdict(int)
                self.hwm[s] = max(self.hwm[s], v)
            self.sem_names.add(s)
            if self.waited[e][s] < v:
                self.waited[e][s] = v
                out.append((s, v))
        return out

    def op(self, e, build, waits=None, inc="cp", inc_n=1):
        if inc == "cp":
            inc = "cp" + e
        w = self.need(e, waits)
        tk = self.tick(inc, inc_n) if inc else None

        def fn(eng, SS, _w=w, _tk=tk, _n=inc_n, _b=build):
            for (s, v) in _w:
                eng.wait_ge(SS[s], v)
            ins = _b(eng, SS)
            if _tk is not None:
                ins.then_inc(SS[_tk[0]], _n)
        self.ops[e].append(fn)
        return tk

    def raw(self, e, fn):
        self.ops[e].append(fn)

    def dma(self, e, out, in_, waits=None, sem=None, accum=False):
        if sem is None:
            sem = "dmah" if e in ("sy", "sc") else "dmas"
        elif not sem.startswith("dm"):
            sem = "dm" + sem
        if not hasattr(self, "hwm"):
            self.hwm = defaultdict(int)
        w = list(waits or [])
        if self.hwm[sem] > 0:
            w.append((sem + "#hwm", self.hwm[sem]))
        def build(eng, SS, _o=out, _i=in_, _a=accum):
            if _a:
                return eng.dma_start(out=_o, in_=_i, accum_op=ALU.add)
            return eng.dma_start(out=_o, in_=_i)
        return self.op(e, build, waits=w, inc=sem, inc_n=16)

    def barrier(self, engines=ENGS):
        tks = {}
        for e in engines:
            tks[e] = self.tick("bar" + e, 1)

            def fn(eng, SS, _e=e):
                eng.sem_inc(SS["bar" + _e], 1)
            self.ops[e].append(fn)
        dma_snap = [(s, self.cnt[s]) for s in list(self.cnt) if s.startswith("dm")
                    and self.cnt[s] > 0]
        for e in engines:
            w = self.need(e, [tks[f] for f in engines if f != e] + dma_snap)

            def fn(eng, SS, _w=w):
                for (s, v) in _w:
                    eng.wait_ge(SS[s], v)
            self.ops[e].append(fn)

    def mm(self, bi, lhsT, rhs, start, stop, waits=None, n=512, extra_inc=None,
           op=128):
        w = list(waits or [])
        if start and self.ev_tk[bi] is not None:
            w.append(self.ev_tk[bi])
        w = self.need("te", w)
        tk = self.tick("mm", 1) if stop else None
        tk2 = self.tick(extra_inc, 1) if extra_inc else None

        def fn(eng, SS, _w=w, _l=lhsT, _r=rhs, _s=start, _e=stop, _tk=tk, _tk2=tk2,
               _bi=bi, _n=n, _op=op):
            for (s, v) in _w:
                eng.wait_ge(SS[s], v)
            ins = eng.matmul(out=self.pt[_bi][0:_op, :_n], lhsT=_l, rhs=_r,
                             start=_s, stop=_e)
            if _tk is not None and _tk2 is not None:
                ins.then_inc(SS["mm"], 1)
                eng.nop().then_inc(SS[_tk2[0]], 1)
            elif _tk is not None:
                ins.then_inc(SS["mm"], 1)
            elif _tk2 is not None:
                ins.then_inc(SS[_tk2[0]], 1)
        self.ops["te"].append(fn)
        if stop:
            self.mm_tk[bi] = tk
        return tk if stop else tk2

    def transpose(self, bi, in_ap, ident_ap, waits=None, n=128, extra_inc=None,
                  op=128):
        w = list(waits or [])
        if self.ev_tk[bi] is not None:
            w.append(self.ev_tk[bi])
        w = self.need("te", w)
        tk = self.tick("mm", 1)
        tk2 = self.tick(extra_inc, 1) if extra_inc else None

        def fn(eng, SS, _w=w, _i=in_ap, _id=ident_ap, _tk=tk, _bi=bi, _n=n,
               _tk2=tk2, _op=op):
            for (s, v) in _w:
                eng.wait_ge(SS[s], v)
            ins = eng.transpose(out=self.pt[_bi][0:_op, :_n], in_=_i, identity=_id)
            ins.then_inc(SS["mm"], 1)
            if _tk2 is not None:
                eng.nop().then_inc(SS[_tk2[0]], 1)
        self.ops["te"].append(fn)
        self.mm_tk[bi] = tk
        return tk

    def evac(self, bi, e, build, waits=None):
        w = list(waits or [])
        if self.mm_tk[bi] is not None:
            w.append(self.mm_tk[bi])
        w = self.need(e, w)
        tk = self.tick("ev", 1)

        def fn(eng, SS, _w=w, _b=build, _bi=bi):
            for (s, v) in _w:
                eng.wait_ge(SS[s], v)
            ins = _b(eng, SS, self.pt[_bi])
            ins.then_inc(SS["ev"], 1)
        self.ops[e].append(fn)
        self.ev_tk[bi] = tk
        return tk

    def finalize(self):
        nc = self.nc
        SS = {}
        for s in sorted(self.sem_names):
            SS[s] = self.stack.enter_context(nc.semaphore(s))
        block = self.stack.enter_context(nc.Block())
        emap = dict(sy=block.sync, te=block.tensor, ve=block.vector,
                    sc=block.scalar, gp=block.gpsimd)
        for e in ENGS:
            ops = self.ops[e]

            def run(eng, _ops=ops):
                for f in _ops:
                    f(eng, SS)
            emap[e](run)


def prep_w(W, ow=128):
    n_in, n_out = W.shape
    nch = n_in // 128
    ns = n_out // ow
    return np.ascontiguousarray(
        W.reshape(nch, 128, ns, ow).transpose(2, 1, 0, 3)).astype(np.float32)


def arenas(ctx):
    sb = {}
    sb["X"] = [ctx.sbuf(f"xa{i}", [128, 2048]) for i in range(NCH)]
    for i in range(5):
        sb[f"A{i}"] = ctx.sbuf(f"aa{i}", [128, 2048])
    sb["W0"] = ctx.sbuf("wa0", [128, 2048])
    sb["W1"] = ctx.sbuf("wa1", [128, 2048])
    sb["ones128"] = ctx.sbuf("ones128", [128, 128])
    sb["ident"] = ctx.sbuf("ident", [128, 128])
    sb["sm0"] = ctx.sbuf("sm0", [128, 512])
    sb["sm1"] = ctx.sbuf("sm1", [128, 512])
    return sb


def load_const(ctx, sb):
    ones = ctx.input("ones128_in", lambda c: np.ones((128, 128), np.float32), [128, 128])
    ident = ctx.input("ident_in", lambda c: np.eye(128, dtype=np.float32), [128, 128])
    t1 = ctx.dma("sy", sb["ones128"][:], ones[:, :])
    t2 = ctx.dma("sy", sb["ident"][:], ident[:, :])
    sb["onesr"] = ctx.sbuf("onesr", [1, 512])
    ctx.op("ve", lambda eng, SS: eng.memset(sb["onesr"][:], 1.0))
    return [t1, t2]


def emit_ln(ctx, sb, resid_dram, affine=None):
    X = sb["X"]
    SQ = [sb["A0"], sb["A1"]]
    MU, M2 = sb["A2"], sb["A3"]
    ones = sb["ones128"]

    ld = [ctx.dma("sy", X[c][:], resid_dram[c * 128:(c + 1) * 128, :]) for c in range(NCH)]

    sq_tk = []
    sqf0 = ctx.cnt["sqf"]
    for c in range(NCH):
        w = [ld[c]]
        if c >= 2:
            w.append(("sqf", sqf0 + c - 1))
        sq_tk.append(ctx.op("sc", lambda eng, SS, c=c: eng.activation(
            out=SQ[c % 2][:], in_=X[c][:], func=ACT.Square), waits=w))

    for c in range(NCH):
        for tt in range(NTT):
            ctx.mm(tt, ones[:], X[c][:, tt * 512:(tt + 1) * 512],
                   start=(c == 0), stop=(c == NCH - 1), waits=[ld[c]])
        for tt in range(NTT):
            ctx.mm(4 + tt, ones[:], SQ[c % 2][:, tt * 512:(tt + 1) * 512],
                   start=(c == 0), stop=(c == NCH - 1), waits=[sq_tk[c]],
                   extra_inc=("sqf" if tt == NTT - 1 else None))

    for tt in range(NTT):
        ctx.evac(tt, "sc", lambda eng, SS, pt, tt=tt: eng.activation(
            out=MU[:, tt * 512:(tt + 1) * 512], in_=pt[:, :512], func=ACT.Copy,
            scale=1.0 / D))
        ctx.evac(4 + tt, "sc", lambda eng, SS, pt, tt=tt: eng.activation(
            out=M2[:, tt * 512:(tt + 1) * 512], in_=pt[:, :512], func=ACT.Copy,
            scale=1.0 / D))
    last_ev = ("ev", ctx.cnt["ev"])
    t = ctx.op("ve", lambda eng, SS: eng.tensor_tensor(
        out=SQ[0][:], in0=MU[:], in1=MU[:], op=ALU.mult), waits=[last_ev])
    t = ctx.op("ve", lambda eng, SS: eng.tensor_tensor(
        out=SQ[0][:], in0=M2[:], in1=SQ[0][:], op=ALU.subtract), waits=[t])
    epsb, epstk = ctx.constf(EPS)
    t = ctx.op("sc", lambda eng, SS: eng.activation(
        out=SQ[1][:], in_=SQ[0][:], func=ACT.Ln, bias=epsb[:, 0:1]), waits=[t, epstk])
    t = ctx.op("sc", lambda eng, SS: eng.activation(
        out=M2[:], in_=SQ[1][:], func=ACT.Exp, scale=-0.5), waits=[t])
    nmu = ctx.op("ve", lambda eng, SS: eng.scalar_tensor_tensor(
        out=MU[:], in0=MU[:], scalar=-1.0, in1=M2[:], op0=ALU.mult, op1=ALU.mult),
        waits=[t])

    out_tk = []
    for c in range(NCH):
        e = "ve" if c % 2 == 0 else "gp"
        t1 = ctx.op(e, lambda eng, SS, c=c: eng.tensor_tensor(
            out=X[c][:], in0=X[c][:], in1=M2[:], op=ALU.mult), waits=[nmu])
        t2 = ctx.op(e, lambda eng, SS, c=c: eng.tensor_tensor(
            out=X[c][:], in0=X[c][:], in1=MU[:], op=ALU.add), waits=[t1])
        if affine is not None:
            g, bta = affine
            t2 = ctx.op(e, lambda eng, SS, c=c: eng.tensor_scalar(
                out=X[c][:], in0=X[c][:], scalar1=g[c], scalar2=bta[c],
                op0=ALU.mult, op1=ALU.add), waits=[t2])
        out_tk.append(t2)
    fin = ctx.op("ve", lambda eng, SS: eng.engine_nop(),
                 waits=[out_tk[-1], out_tk[-2]])
    out_tk.append(fin)
    return out_tk


def emit_proj(ctx, sb, w_dram, n_in, n_out, in_tiles, in_tks, evac_cb, ow=128,
              lhsT_hook=None):
    nch_in = n_in // 128
    ns = n_out // ow
    assert ow == 128
    wtiles = [sb["W0"], sb["W1"]]
    prev_mm = [None, None]
    for s in range(ns):
        wt = wtiles[s % 2]
        w = list(in_tks or [])
        if prev_mm[s % 2] is not None:
            w.append(prev_mm[s % 2])
        ldtk = ctx.dma("sy", wt[:, :nch_in * ow],
                       w_dram[s].rearrange("a b c -> a (b c)"), waits=w)
        bi_base = (s % 2) * 4
        last = None
        for tt in range(NTT):
            bi = bi_base + tt
            for dc in range(nch_in):
                lhsT = wt[:, dc * ow: dc * ow + 128]
                if lhsT_hook is not None:
                    lhsT = lhsT_hook(wt, dc, s)
                rhs = in_tiles[dc][:, tt * 512:(tt + 1) * 512]
                last = ctx.mm(bi, lhsT, rhs, start=(dc == 0),
                              stop=(dc == nch_in - 1),
                              waits=([ldtk] if dc == 0 else None))
            evac_cb(ctx, bi, s, tt, wt)
        prev_mm[s % 2] = last


def resid_evac_maker(ctx, sb, resid_dram, scale, n_out=D):
    R = [sb["A2"], sb["A3"]]
    state = {"ld": {}, "st": {}}

    def cb(ctx, bi, oc, tt, wt):
        r = R[oc % 2]
        if tt == 0:
            w = [state["st"].get(oc - 2)]
            state["ld"][oc] = ctx.dma("sy", r[:],
                                      resid_dram[oc * 128:(oc + 1) * 128, :], waits=w)
        ev = ctx.evac(bi, "ve", lambda eng, SS, pt, oc=oc, tt=tt, r=r:
                      eng.scalar_tensor_tensor(
                          out=r[:, tt * 512:(tt + 1) * 512], in0=pt[:, :512],
                          scalar=float(scale), in1=r[:, tt * 512:(tt + 1) * 512],
                          op0=ALU.mult, op1=ALU.add),
                      waits=[state["ld"][oc]])
        if tt == NTT - 1:
            state["st"][oc] = ctx.dma("gp", resid_dram[oc * 128:(oc + 1) * 128, :],
                                      r[:], waits=[ev])
    return cb, state


def emit_l2_ffn(ctx, sb, resid_dram, core, host, dbg=None):
    nc = ctx.nc
    X = sb["X"]
    wq_pack = ctx.input("L2w_pack", lambda c: prep_w(host["L2_ffn_weight"].T.copy()),
                        [16, 128, 16, 128])
    gam = ctx.input("L2g", lambda c: np.ascontiguousarray(
        host["L2_ffn_ln_gamma"].reshape(16, 128).T.astype(np.float32)), [128, 16])
    bet = ctx.input("L2b", lambda c: np.ascontiguousarray(
        host["L2_ffn_ln_beta"].reshape(16, 128).T.astype(np.float32)), [128, 16])
    bias = ctx.input("L2bias", lambda c: np.ascontiguousarray(
        host["L2_ffn_bias"].reshape(16, 128).T.astype(np.float32)), [128, 16])
    gam_sb = ctx.sbuf("l2gam", [128, 16])
    bet_sb = ctx.sbuf("l2bet", [128, 16])
    bias_sb = ctx.sbuf("l2bias", [128, 16])
    thr_sb = ctx.sbuf("l2thr", [128, 2])
    tg = ctx.dma("sy", gam_sb[:], gam[:, :])
    tb = ctx.dma("sy", bet_sb[:], bet[:, :])
    tbias = ctx.dma("sy", bias_sb[:], bias[:, :])

    SQ = [sb["A0"], sb["A1"]]
    wtiles = [sb["W0"], sb["W1"]]
    ones = sb["ones128"]
    prev = [None, None]
    for s in range(16):
        wt = wtiles[s % 2]
        w = [prev[s % 2]] if prev[s % 2] else None
        ldtk = ctx.dma("sy", wt[:, :2048], wq_pack[s].rearrange("a b c -> a (b c)"),
                       waits=w)
        ab = ctx.op("sc", lambda eng, SS, wt=wt, s=s: eng.activation(
            out=SQ[s % 2][:], in_=wt[:, :2048], func=ACT.Abs), waits=[ldtk])
        for tt in range(NTT):
            tk = ctx.mm(tt, ones[:], SQ[s % 2][:, tt * 512:(tt + 1) * 512],
                        start=(s == 0), stop=(s == 15), waits=[ab], n=512,
                        extra_inc=("p1f" if tt == NTT - 1 else None))
            if tt == NTT - 1:
                prev[s % 2] = tk
    red = []
    for tt in range(NTT):
        ctx.evac(tt, "ve", lambda eng, SS, pt, tt=tt: eng.tensor_copy(
            out=SQ[0][:, tt * 512:(tt + 1) * 512], in_=pt[:, :512]))
    t = ctx.op("ve", lambda eng, SS: eng.tensor_reduce(
        out=SQ[1][:, 0:1], in_=SQ[0][:, 0:2048], axis=AX.X, op=ALU.add),
        waits=[("ev", ctx.cnt["ev"])])
    t = ctx.op("ve", lambda eng, SS: eng.tensor_scalar(
        out=thr_sb[:, 0:1], in0=SQ[1][:, 0:1], scalar1=0.5 / (2048.0 * 2048.0),
        scalar2=0.5 * EPS, op0=ALU.mult, op1=ALU.add), waits=[t])
    thr_tk = ctx.op("ve", lambda eng, SS: eng.tensor_scalar(
        out=thr_sb[:, 1:2], in0=thr_sb[:, 0:1], scalar1=-1.0, scalar2=None,
        op0=ALU.mult), waits=[t])

    if dbg == "thr":
        for c in range(NCH):
            ctx.dma("gp", resid_dram[c * 128:(c + 1) * 128, 0:2], thr_sb[:, 0:2],
                    waits=[thr_tk])
        return
    ctx.barrier()
    gaff = [gam_sb[:, c:c + 1] for c in range(16)]
    baff = [bet_sb[:, c:c + 1] for c in range(16)]
    xn_tk = emit_ln(ctx, sb, resid_dram, affine=(gaff, baff))

    if dbg == "xn":
        for c in range(NCH):
            ctx.dma("gp", resid_dram[c * 128:(c + 1) * 128, :], sb["X"][c][:],
                    waits=[xn_tk[c]])
        return
    bias_half = ctx.sbuf("l2biash", [128, 16])
    bh_tk = ctx.op("ve", lambda eng, SS: eng.tensor_scalar(
        out=bias_half[:], in0=bias_sb[:], scalar1=0.5, scalar2=None, op0=ALU.mult),
        waits=[tbias])

    rst = {"ld": {}, "st": {}}
    R = [sb["A2"], sb["A3"]]

    def cb(ctx, bi, oc, tt):
        r = R[oc % 2]
        if tt == 0:
            w = [rst["st"].get(oc - 2), xn_tk[-1]]
            rst["ld"][oc] = ctx.dma("sy", r[:],
                                    resid_dram[oc * 128:(oc + 1) * 128, :], waits=w)
        ev = ctx.evac(bi, "ve", lambda eng, SS, pt, oc=oc, tt=tt, r=r:
                      eng.scalar_tensor_tensor(
                          out=r[:, tt * 512:(tt + 1) * 512], in0=pt[:, :512],
                          scalar=0.5, in1=r[:, tt * 512:(tt + 1) * 512],
                          op0=ALU.mult, op1=ALU.add),
                      waits=[rst["ld"][oc]])
        if tt == NTT - 1:
            ev = ctx.op("ve", lambda eng, SS, r=r, oc=oc: eng.tensor_scalar(
                out=r[:], in0=r[:], scalar1=bias_half[:, oc:oc + 1], scalar2=None,
                op0=ALU.add), waits=[ev, bh_tk])
            rst["st"][oc] = ctx.dma("gp", resid_dram[oc * 128:(oc + 1) * 128, :],
                                    r[:], waits=[ev])

    nch_in = 16
    wtiles2 = [sb["W0"], sb["W1"]]
    prev_mm = [None, None]
    for s in range(16):
        wt = wtiles2[s % 2]
        w = [xn_tk[-1], thr_tk]
        if prev_mm[s % 2] is not None:
            w.append(prev_mm[s % 2])
        ldtk = ctx.dma("sy", wt[:, :2048], wq_pack[s].rearrange("a b c -> a (b c)"),
                       waits=w)
        scr = [sb["A0"], sb["A1"]][s % 2]
        if dbg == "nq":
            q2 = ldtk
        else:
            q1 = ctx.op("ve", lambda eng, SS, wt=wt, scr=scr: eng.tensor_scalar(
                out=scr[:], in0=wt[:, :2048], scalar1=thr_sb[:, 0:1], scalar2=1.0,
                op0=ALU.is_gt, op1=ALU.subtract), waits=[ldtk])
            q2 = ctx.op("ve", lambda eng, SS, wt=wt, scr=scr: eng.scalar_tensor_tensor(
                out=wt[:, :2048], in0=wt[:, :2048], scalar=thr_sb[:, 1:2],
                op0=ALU.is_ge, in1=scr[:], op1=ALU.add), waits=[q1])
        bi_base = (s % 2) * 4
        for tt in range(NTT):
            bi = bi_base + tt
            for dc in range(nch_in):
                lhsT = wt[:, dc * 128:(dc + 1) * 128]
                rhs = X[dc][:, tt * 512:(tt + 1) * 512]
                prev_mm[s % 2] = ctx.mm(bi, lhsT, rhs, start=(dc == 0),
                                        stop=(dc == nch_in - 1),
                                        waits=([q2] if dc == 0 else None))
            cb(ctx, bi, s, tt)
        prev_mm[s % 2] = ctx.ev_tk[bi_base + NTT - 1]
        if dbg == "slow":
            ctx.barrier()
    return



def emit_weight_rms_thr(ctx, sb, wpack, ns, name):
    SQ = [sb["A0"], sb["A1"]]
    wtiles = [sb["W0"], sb["W1"]]
    ones = sb["ones128"]
    n_el = ns * 128 * 2048
    prev = [None, None]
    for s in range(ns):
        wt = wtiles[s % 2]
        w = [prev[s % 2]] if prev[s % 2] else None
        ldtk = ctx.dma("sy", wt[:, :2048], wpack[s].rearrange("a b c -> a (b c)"),
                       waits=w)
        sqk = ctx.op("sc", lambda eng, SS, wt=wt, s=s: eng.activation(
            out=SQ[s % 2][:], in_=wt[:, :2048], func=ACT.Square), waits=[ldtk])
        for tt in range(NTT):
            tk = ctx.mm(tt, ones[:], SQ[s % 2][:, tt * 512:(tt + 1) * 512],
                        start=(s == 0), stop=(s == ns - 1), waits=[sqk], n=512,
                        extra_inc=("p1f" if tt == NTT - 1 else None))
            if tt == NTT - 1:
                prev[s % 2] = tk
    for tt in range(NTT):
        ctx.evac(tt, "ve", lambda eng, SS, pt, tt=tt: eng.tensor_copy(
            out=SQ[0][:, tt * 512:(tt + 1) * 512], in_=pt[:, :512]))
    t = ctx.op("ve", lambda eng, SS: eng.tensor_reduce(
        out=SQ[1][:, 0:1], in_=SQ[0][:, 0:2048], axis=AX.X, op=ALU.add),
        waits=[("ev", ctx.cnt["ev"])])
    scl = ctx.sbuf(name + "_scl", [128, 4])
    t = ctx.op("ve", lambda eng, SS: eng.tensor_scalar(
        out=scl[:, 1:2], in0=SQ[1][:, 0:1], scalar1=1.0 / n_el, scalar2=float(EPS),
        op0=ALU.mult, op1=ALU.max), waits=[t])
    t = ctx.op("sc", lambda eng, SS: eng.activation(
        out=scl[:, 2:3], in_=scl[:, 1:2], func=ACT.Ln), waits=[t])
    t = ctx.op("sc", lambda eng, SS: eng.activation(
        out=scl[:, 0:1], in_=scl[:, 2:3], func=ACT.Exp, scale=0.5), waits=[t])
    t = ctx.op("ve", lambda eng, SS: eng.reciprocal(
        out=scl[:, 3:4], in_=scl[:, 0:1]), waits=[t])
    t = ctx.op("ve", lambda eng, SS: eng.tensor_tensor(
        out=scl[:, 3:4], in0=scl[:, 3:4], in1=scl[:, 1:2], op=ALU.mult), waits=[t])
    t = ctx.op("ve", lambda eng, SS: eng.tensor_tensor(
        out=scl[:, 0:1], in0=scl[:, 0:1], in1=scl[:, 3:4], op=ALU.add), waits=[t])
    t = ctx.op("ve", lambda eng, SS: eng.tensor_scalar(
        out=scl[:, 0:1], in0=scl[:, 0:1], scalar1=0.5, scalar2=None, op0=ALU.mult),
        waits=[t])
    thr = ctx.sbuf(name + "_thr", [128, 2])
    t1 = ctx.op("ve", lambda eng, SS: eng.tensor_scalar(
        out=thr[:, 0:1], in0=scl[:, 0:1], scalar1=0.5, scalar2=None, op0=ALU.mult),
        waits=[t])
    t1 = ctx.op("ve", lambda eng, SS: eng.tensor_scalar(
        out=thr[:, 1:2], in0=thr[:, 0:1], scalar1=-1.0, scalar2=None, op0=ALU.mult),
        waits=[t1])
    return scl, thr, t1


def emit_ladder(ctx, x_ap, scb_ap, u_ap, u0_ap, out_ap, waits, out_dtype_cast=False):
    t = ctx.op("ve", lambda eng, SS: eng.tensor_tensor(
        out=u_ap, in0=x_ap, in1=scb_ap, op=ALU.mult), waits=waits)
    t = ctx.op("ve", lambda eng, SS: eng.tensor_scalar(
        out=u0_ap, in0=u_ap, scalar1=128.5, scalar2=None, op0=ALU.add), waits=[t])
    t = ctx.op("ve", lambda eng, SS: eng.tensor_copy(out=u_ap, in_=u0_ap), waits=[t])
    for k in range(7, -1, -1):
        c = float(2 ** k)
        t = ctx.op("ve", lambda eng, SS, c=c: eng.tensor_scalar(
            out=out_ap, in0=u_ap, scalar1=c, scalar2=c, op0=ALU.is_ge, op1=ALU.mult),
            waits=[t])
        t = ctx.op("ve", lambda eng, SS: eng.scalar_tensor_tensor(
            out=u_ap, in0=out_ap, scalar=-1.0, in1=u_ap, op0=ALU.mult, op1=ALU.add),
            waits=[t])
    t = ctx.op("ve", lambda eng, SS: eng.scalar_tensor_tensor(
        out=u_ap, in0=u_ap, scalar=-1.0, in1=u0_ap, op0=ALU.mult, op1=ALU.add),
        waits=[t])
    t = ctx.op("ve", lambda eng, SS: eng.tensor_scalar(
        out=out_ap, in0=u_ap, scalar1=-128.0, scalar2=None, op0=ALU.add), waits=[t])
    return t


def emit_rowpack_bcast(ctx, sb, srcs, dst_b, name, extra=None):
    dvec = ctx.dram(name + "_vec", [1, T])
    rowt = sb["A4"]
    stks = []
    for b in range(16):
        ap, tk = srcs[b]
        stks.append(ctx.dma("gp", dvec[0:1, b * 128:(b + 1) * 128],
                            ap, waits=[tk]))
    ld = ctx.dma("sy", rowt[0:1, 0:T], dvec[:, :],
                 waits=stks + [("mm", ctx.cnt["mm"])])
    ones = sb["ones128"]
    for tt in range(NTT):
        ctx.mm(tt, ones[0:1, :], rowt[0:1, tt * 512:(tt + 1) * 512],
               start=True, stop=True, waits=[ld])
    evs = []
    for tt in range(NTT):
        evs.append(ctx.evac(tt, "sc", lambda eng, SS, pt, tt=tt: eng.activation(
            out=dst_b[:, tt * 512:(tt + 1) * 512], in_=pt[:, :512], func=ACT.Copy)))
    return evs[-1]


def emit_colmax_tokens(ctx, sb, run_ap, run_tk, name):
    ident = sb["ident"]
    outs = []
    res = ctx.sbuf(name + "_amax", [128, 16])
    for b in range(16):
        bi = b % 8
        ctx.transpose(bi, run_ap[:, b * 128:(b + 1) * 128], ident[:], waits=[run_tk])
        tk = ctx.evac(bi, "ve", lambda eng, SS, pt, b=b: eng.tensor_reduce(
            out=res[:, b:b + 1], in_=pt[:, :128], axis=AX.X, op=ALU.max))
        outs.append((res[:, b:b + 1], tk))
    return outs


def emit_l1_ffn(ctx, sb, resid_dram, core, host, lname="L1", dbg=None):
    X = sb["X"]
    w1p = ctx.input(lname + "w1p", lambda c: prep_w(host["L1_ffn_w1"].T.copy()),
                    [64, 128, 16, 128])
    def w2prep(c):
        w2t = host["L1_ffn_w2"].T.copy()
        a = w2t.reshape(4, 16, 128, 16, 128)
        return np.ascontiguousarray(a.transpose(3, 0, 2, 1, 4)).astype(np.float32)
    w2p = ctx.input(lname + "w2p", w2prep, [16, 4, 128, 16, 128])
    h_dram = ctx.dram(lname + "_h", [8192, T])
    nh_dram = ctx.dram(lname + "_nh", [8192, T], BF16)
    q2_dram = ctx.dram(lname + "_q2", [16, 4, 128, 2048], BF16)

    scl1, thr1, tk1 = emit_weight_rms_thr(ctx, sb, w1p, 64, lname + "r1")
    ctx.barrier()
    w2flat = w2p.rearrange("a b c d e -> (a b) c d e")
    scl2, thr2, tk2 = emit_weight_rms_thr(ctx, sb, w2flat, 64, lname + "r2")
    ctx.barrier()
    wtiles = [sb["W0"], sb["W1"]]
    SQ = [sb["A0"], sb["A1"]]
    prev = [None, None]
    for i in range(64):
        s, g = i // 4, i % 4
        wt = wtiles[i % 2]
        w = [tk2] + ([prev[i % 2]] if prev[i % 2] else [])
        ldtk = ctx.dma("sy", wt[:, :2048],
                       w2p[s, g].rearrange("a b c -> a (b c)"), waits=w)
        q1 = ctx.op("ve", lambda eng, SS, wt=wt, i=i: eng.tensor_scalar(
            out=SQ[i % 2][:], in0=wt[:, :2048], scalar1=thr2[:, 0:1], scalar2=1.0,
            op0=ALU.is_gt, op1=ALU.subtract), waits=[ldtk])
        q2 = ctx.op("ve", lambda eng, SS, wt=wt, i=i: eng.scalar_tensor_tensor(
            out=SQ[i % 2][:], in0=wt[:, :2048], scalar=thr2[:, 1:2],
            op0=ALU.is_ge, in1=SQ[i % 2][:], op1=ALU.add), waits=[q1])
        prev[i % 2] = ctx.dma("gp", q2_dram[s, g], SQ[i % 2][:], waits=[q2])
    ctx.barrier()

    xn_tk = emit_ln(ctx, sb, resid_dram)

    run = sb["A2"]
    t = ctx.op("ve", lambda eng, SS: eng.memset(run[:], 0.0), waits=[xn_tk[-1]])
    for c in range(NCH):
        t = ctx.op("ve", lambda eng, SS, c=c: eng.tensor_tensor(
            out=run[:], in0=run[:], in1=X[c][:], op=ALU.max), waits=[t])
        t = ctx.op("ve", lambda eng, SS, c=c: eng.scalar_tensor_tensor(
            out=run[:], in0=X[c][:], scalar=-1.0, in1=run[:],
            op0=ALU.mult, op1=ALU.max), waits=[t])
    amaxs = emit_colmax_tokens(ctx, sb, run, t, lname + "a")
    SCB = sb["A2"]
    btk = emit_rowpack_bcast(ctx, sb, amaxs, SCB, lname + "sca")
    RS1B = sb["A3"]
    t = ctx.op("ve", lambda eng, SS: eng.tensor_single_scalar(
        out=SCB[:], in_=SCB[:], scalar=float(EPS), op=ALU.max), waits=[btk])
    zvec = ctx.sbuf(lname + "_zv", [128, 1])
    t2 = ctx.op("ve", lambda eng, SS: eng.tensor_scalar(
        out=zvec[:], in0=scl1[:, 0:1], scalar1=1.0 / 127.0, scalar2=None,
        op0=ALU.mult), waits=[tk1])
    t2 = ctx.op("ve", lambda eng, SS: eng.tensor_scalar(
        out=RS1B[:], in0=SCB[:], scalar1=zvec[:, 0:1], scalar2=None,
        op0=ALU.mult), waits=[t, t2])
    t = ctx.op("ve", lambda eng, SS: eng.reciprocal(out=SCB[:], in_=SCB[:]),
        waits=[t2])
    t = ctx.op("ve", lambda eng, SS: eng.tensor_scalar(
        out=SCB[:], in0=SCB[:], scalar1=127.0, scalar2=None, op0=ALU.mult),
        waits=[t])
    u, u0 = sb["A0"], sb["A1"]
    na_tk = []
    for c in range(NCH):
        tk = emit_ladder(ctx, X[c][:], SCB[:], u[:], u0[:], X[c][:],
                         [t] if c == 0 else [t, na_tk[-1]])
        na_tk.append(tk)

    if dbg == "na":
        for c in range(NCH):
            ctx.dma("gp", resid_dram[c * 128:(c + 1) * 128, :], X[c][:],
                    waits=[na_tk[-1]])
        return
    if dbg == "sca":
        for c in range(NCH):
            ctx.dma("gp", resid_dram[c * 128:(c + 1) * 128, :], SCB[:], waits=[t])
        return
    stg = [sb["A1"], sb["A2"]]
    runh = sb["A0"]
    prev = [None, None]
    hst = {}
    zt = ctx.op("ve", lambda eng, SS: eng.memset(runh[:], 0.0), waits=[na_tk[-1]])
    for s in range(64):
        wt = wtiles[s % 2]
        w = [tk1, na_tk[-1]] + ([prev[s % 2]] if prev[s % 2] else [])
        ldtk = ctx.dma("sy", wt[:, :2048], w1p[s].rearrange("a b c -> a (b c)"),
                       waits=w)
        q1 = ctx.op("ve", lambda eng, SS, wt=wt, s=s: eng.tensor_scalar(
            out=sb["sm0" if s % 2 == 0 else "sm1"][:, 0:512] if False else wt[:, :2048],
            in0=wt[:, :2048], scalar1=thr1[:, 0:1], scalar2=1.0,
            op0=ALU.is_gt, op1=ALU.subtract), waits=[ldtk]) if False else None
        qs = sb["A4"]
        q1 = ctx.op("ve", lambda eng, SS, wt=wt, qs=qs: eng.tensor_scalar(
            out=qs[:], in0=wt[:, :2048], scalar1=thr1[:, 0:1], scalar2=1.0,
            op0=ALU.is_gt, op1=ALU.subtract), waits=[ldtk])
        q2 = ctx.op("ve", lambda eng, SS, wt=wt, qs=qs: eng.scalar_tensor_tensor(
            out=wt[:, :2048], in0=wt[:, :2048], scalar=thr1[:, 1:2],
            op0=ALU.is_ge, in1=qs[:], op1=ALU.add), waits=[q1])
        st = stg[s % 2]
        bi_base = (s % 2) * 4
        mm_last = None
        for tt in range(NTT):
            bi = bi_base + tt
            for dc in range(NCH):
                mm_last = ctx.mm(bi, wt[:, dc * 128:(dc + 1) * 128],
                                 X[dc][:, tt * 512:(tt + 1) * 512],
                                 start=(dc == 0), stop=(dc == NCH - 1),
                                 waits=([q2] if dc == 0 else None))
            ctx.evac(bi, "ve", lambda eng, SS, pt, st=st, tt=tt: eng.tensor_tensor(
                out=st[:, tt * 512:(tt + 1) * 512], in0=pt[:, :512],
                in1=RS1B[:, tt * 512:(tt + 1) * 512], op=ALU.mult),
                waits=[hst.get(s - 2)])
        glu = ctx.op("sc", lambda eng, SS, st=st: eng.activation(
            out=st[:], in_=st[:], func=ACT.Gelu), waits=[("ev", ctx.cnt["ev"])])
        am = ctx.op("ve", lambda eng, SS, st=st: eng.tensor_tensor(
            out=runh[:], in0=runh[:], in1=st[:], op=ALU.max), waits=[glu, zt])
        am = ctx.op("ve", lambda eng, SS, st=st: eng.scalar_tensor_tensor(
            out=runh[:], in0=st[:], scalar=-1.0, in1=runh[:],
            op0=ALU.mult, op1=ALU.max), waits=[am])
        hst[s] = ctx.dma("gp", h_dram[s * 128:(s + 1) * 128, :], st[:], waits=[am])
        prev[s % 2] = mm_last
    ctx.barrier()

    if dbg == "h":
        for c in range(NCH):
            ctx.dma("gp", resid_dram[c * 128:(c + 1) * 128, :],
                    h_dram[c * 128:(c + 1) * 128, :],
                    waits=[hst[NCH - 1], ("dmas", ctx.cnt["dmas"])])
        return
    amaxh = emit_colmax_tokens(ctx, sb, runh, ("cpve", ctx.cnt["cpve"]), lname + "h")
    SCHB = sb["A2"]
    btk = emit_rowpack_bcast(ctx, sb, amaxh, SCHB, lname + "sch")
    RS2B = sb["A3"]
    t = ctx.op("ve", lambda eng, SS: eng.tensor_single_scalar(
        out=SCHB[:], in_=SCHB[:], scalar=float(EPS), op=ALU.max), waits=[btk])
    zvec2 = ctx.sbuf(lname + "_zv2", [128, 1])
    t2 = ctx.op("ve", lambda eng, SS: eng.tensor_scalar(
        out=zvec2[:], in0=scl2[:, 0:1], scalar1=1.0 / 254.0, scalar2=None,
        op0=ALU.mult), waits=[tk2])
    t2 = ctx.op("ve", lambda eng, SS: eng.tensor_scalar(
        out=RS2B[:], in0=SCHB[:], scalar1=zvec2[:, 0:1], scalar2=None,
        op0=ALU.mult), waits=[t, t2])
    t = ctx.op("ve", lambda eng, SS: eng.reciprocal(out=SCHB[:], in_=SCHB[:]),
        waits=[t2])
    t = ctx.op("ve", lambda eng, SS: eng.tensor_scalar(
        out=SCHB[:], in0=SCHB[:], scalar1=127.0, scalar2=None, op0=ALU.mult),
        waits=[t])

    if dbg == "sch":
        for c in range(NCH):
            ctx.dma("gp", resid_dram[c * 128:(c + 1) * 128, :], SCHB[:], waits=[t])
        return
    ch = [sb["W0"], sb["W1"]]
    prev = [None, None]
    for s in range(64):
        wt = ch[s % 2]
        w = [t] + ([prev[s % 2]] if prev[s % 2] else [])
        ldtk = ctx.dma("sy", wt[:, :2048], h_dram[s * 128:(s + 1) * 128, :], waits=w)
        tk = emit_ladder(ctx, wt[:, :2048], SCHB[:], u[:], u0[:], wt[:, :2048], [ldtk])
        prev[s % 2] = ctx.dma("gp", nh_dram[s * 128:(s + 1) * 128, :],
                              wt[:, :2048], waits=[tk])
    ctx.barrier()

    if dbg == "nh":
        for c in range(NCH):
            w2_ = [prev[0], prev[1], ("dmas", ctx.cnt["dmas"])]
            tk_ = ctx.dma("sy", sb["W0"][:].bitcast(BF16)[:, 0:2048],
                          nh_dram[c * 128:(c + 1) * 128, :], waits=w2_)
            cp_ = ctx.op("ve", lambda eng, SS: eng.tensor_copy(
                out=sb["A4"][:, 0:2048], in_=sb["W0"][:].bitcast(BF16)[:, 0:2048]),
                waits=[tk_])
            ctx.dma("gp", resid_dram[c * 128:(c + 1) * 128, :], sb["A4"][:, 0:2048],
                    waits=[cp_])
            ctx.barrier()
        return
    R = [sb["A2"], sb["A0"]]
    scr = sb["A1"]
    rst = {"ld": {}, "st": {}}
    Xb = [X[c][:].bitcast(BF16)[:, 0:2048] for c in range(NCH)]
    prev_w = None
    for s in range(16):
        r = R[s % 2]
        wld = rst["st"].get(s - 2)
        rst["ld"][s] = ctx.dma("sy", r[:], resid_dram[s * 128:(s + 1) * 128, :],
                               waits=[wld])
        for g in range(4):
            wt = wtiles[(s * 4 + g) % 2]
            w = [prev_w] if prev_w else []
            wtk = ctx.dma("sy", wt[:].bitcast(BF16)[:, 0:2048], q2_dram[s, g], waits=w)
            ntks = []
            for dc in range(NCH):
                ntks.append(ctx.dma("sy", Xb[dc], nh_dram[(g * 16 + dc) * 128:
                                                          (g * 16 + dc + 1) * 128, :],
                                    waits=[prev_w] if prev_w else None))
            bi_base = (s % 2) * 4
            for tt in range(NTT):
                bi = bi_base + tt
                for dc in range(NCH):
                    last = (tt == NTT - 1 and dc == NCH - 1)
                    tkmm = ctx.mm(bi, wt[:].bitcast(BF16)[:, dc * 128:(dc + 1) * 128],
                                  Xb[dc][:, tt * 512:(tt + 1) * 512],
                                  start=(g == 0 and dc == 0),
                                  stop=(g == 3 and dc == NCH - 1),
                                  waits=[wtk, ntks[-1]] if dc == 0 else None,
                                  extra_inc=("gdone" if last else None))
                    if last:
                        prev_w = tkmm
                if g == 3:
                    ev = ctx.evac(bi, "ve", lambda eng, SS, pt, tt=tt, r=r:
                                  eng.tensor_tensor(
                                      out=scr[:, tt * 512:(tt + 1) * 512],
                                      in0=pt[:, :512],
                                      in1=RS2B[:, tt * 512:(tt + 1) * 512],
                                      op=ALU.mult), waits=[rst["ld"][s]])
                    ev = ctx.op("ve", lambda eng, SS, tt=tt, r=r: eng.tensor_tensor(
                        out=r[:, tt * 512:(tt + 1) * 512],
                        in0=r[:, tt * 512:(tt + 1) * 512],
                        in1=scr[:, tt * 512:(tt + 1) * 512], op=ALU.add), waits=[ev])
                    if tt == NTT - 1:
                        rst["st"][s] = ctx.dma("gp",
                                               resid_dram[s * 128:(s + 1) * 128, :],
                                               r[:], waits=[ev])
    return



def emit_proj_slices(ctx, sb, wpack, dst_dram, xn_tk, extra_tbl=None, bias=None,
                     nslices=16, silu=False, carry=None, rope_bias=None,
                     bias_tk=None):
    X = sb["X"]
    wtiles = [sb["W0"], sb["W1"]]
    sta = [sb["A0"], sb["A1"]]
    prev_mm = list(carry["mm"]) if carry else [None, None]
    prev_st = {}
    prev_evs = carry.get("evs") if carry else None
    if carry:
        prev_st[-2], prev_st[-1] = carry["st0"], carry["st1"]
    if extra_tbl is not None:
        wrp, mdr, m2dr = extra_tbl
        mt, mt2 = sb["A2"], sb["A3"]
    if rope_bias is not None:
        rb_sb = ctx.sbuf("rbrow", [1, 256])
    for s in range(nslices):
        wt = wtiles[s % 2]
        st = sta[s % 2]
        w = [xn_tk[-1]]
        if prev_mm[s % 2] is not None:
            w.append(prev_mm[s % 2])
        ldtk = ctx.dma("sy", wt[:, :2048],
                       wpack[s].rearrange("a b c -> a (b c)"), waits=w)
        if extra_tbl is not None:
            wr = sb["A4"]
            w2 = w + (prev_evs[-2:] if prev_evs else [])
            ldr = ctx.dma("sy", wr[:, :2048],
                          wrp[s].rearrange("a b c -> a (b c)"), waits=w2)
            lm = ctx.dma("sy", mt[:], mdr[s * 128:(s + 1) * 128, :], waits=w2)
            lm2 = ctx.dma("sy", mt2[:], m2dr[s * 128:(s + 1) * 128, :], waits=w2)
        if rope_bias is not None:
            brow_d, browr_d = rope_bias
            wb = [prev_mm[0], prev_mm[1]]
            rb_ld = ctx.dma("sy", rb_sb[0:1, 0:128],
                            brow_d[0:1, s * 128:(s + 1) * 128], waits=wb)
            rb_ld2 = ctx.dma("sy", rb_sb[0:1, 128:256],
                             browr_d[0:1, s * 128:(s + 1) * 128], waits=wb)
        bi_base = (s % 2) * 4
        evs = []
        for tt in range(NTT):
            bi = bi_base + tt
            sl = slice(tt * 512, (tt + 1) * 512)
            for dc in range(NCH):
                last = ctx.mm(bi, wt[:, dc * 128:(dc + 1) * 128],
                              X[dc][:, sl],
                              start=(dc == 0),
                              stop=(dc == NCH - 1 and rope_bias is None),
                              waits=([ldtk] if dc == 0 else None))
            if rope_bias is not None:
                last = ctx.mm(bi, rb_sb[0:1, 0:128],
                              sb["onesr"][0:1, 0:512], start=False, stop=True,
                              waits=[rb_ld])
            if extra_tbl is None:
                if bias is not None:
                    evs.append(ctx.evac(bi, "ve", lambda eng, SS, pt, sl=sl, st=st, s=s:
                             eng.tensor_scalar(out=st[:, sl], in0=pt[:, :512],
                                               scalar1=bias[:, s:s + 1],
                                               scalar2=None, op0=ALU.add),
                             waits=[prev_st.get(s - 2), bias_tk]))
                elif silu:
                    evs.append(ctx.evac(bi, "sc", lambda eng, SS, pt, sl=sl, st=st:
                             eng.activation(out=st[:, sl], in_=pt[:, :512],
                                            func=ACT.Silu),
                             waits=[prev_st.get(s - 2)]))
                else:
                    evs.append(ctx.evac(bi, "sc", lambda eng, SS, pt, sl=sl, st=st:
                             eng.activation(out=st[:, sl], in_=pt[:, :512],
                                            func=ACT.Copy),
                             waits=[prev_st.get(s - 2)]))
            else:
                evs.append(ctx.evac(bi, "ve", lambda eng, SS, pt, sl=sl, st=st:
                         eng.tensor_tensor(out=st[:, sl], in0=pt[:, :512],
                                           in1=mt[:, sl], op=ALU.mult),
                         waits=[lm, prev_st.get(s - 2)]))
            prev_mm[s % 2] = last
        if extra_tbl is not None:
            bo = ((s + 1) % 2) * 4
            for tt in range(NTT):
                bi = bo + tt
                sl = slice(tt * 512, (tt + 1) * 512)
                for dc in range(NCH):
                    lastr = ctx.mm(bi, wr[:, dc * 128:(dc + 1) * 128],
                                   X[dc][:, sl], start=(dc == 0),
                                   stop=(dc == NCH - 1 and rope_bias is None),
                                   waits=([ldr] if dc == 0 else None))
                if rope_bias is not None:
                    lastr = ctx.mm(bi, rb_sb[0:1, 128:256],
                                   sb["onesr"][0:1, 0:512], start=False,
                                   stop=True, waits=[rb_ld2])
                e2 = ctx.evac(bi, "ve", lambda eng, SS, pt, sl=sl, s=s:
                              eng.tensor_tensor(out=sta[(s + 1) % 2][:, sl],
                                                in0=pt[:, :512], in1=mt2[:, sl],
                                                op=ALU.mult),
                              waits=[lm2, prev_st.get(s - 1)])
                evs.append(ctx.op("ve", lambda eng, SS, sl=sl, st=st, s=s:
                    eng.tensor_tensor(
                        out=st[:, sl], in0=st[:, sl],
                        in1=sta[(s + 1) % 2][:, sl], op=ALU.add), waits=[e2]))
            prev_mm[(s + 1) % 2] = lastr
        prev_st[s] = ctx.dma("gp", dst_dram[s * 128:(s + 1) * 128, :], st[:],
                             waits=evs)
        prev_evs = evs
    return dict(st0=prev_st[nslices - 2], st1=prev_st[nslices - 1],
                mm=prev_mm, last=prev_st[nslices - 1], evs=prev_evs)


def emit_retention(ctx, sb, resid_dram, core, host, cfg):
    name = cfg["name"]
    nh, dh = cfg["nheads"], cfg["dh"]
    ndc = dh // 128
    HD = nh * dh
    gam = cfg["gammas"]
    CH = 128
    NCHK = T // CH
    X = sb["X"]
    ident = sb["ident"]
    ones = sb["ones128"]

    qT = ctx.dram(name + "_qT", [HD, T])
    kT = ctx.dram(name + "_kT", [HD, T])
    vT = ctx.dram(name + "_vT", [HD, T])
    ktok = ctx.dram(name + "_ktok", [T, HD])
    vtok = ctx.dram(name + "_vtok", [T, HD])
    yT = ctx.dram(name + "_yT", [HD, T])
    stp = ctx.dram(name + "_stp", [HD, dh])
    stg = ctx.dram(name + "_stg", [4 * HD, dh])

    wq = ctx.input(name + "wq", lambda c: prep_w(cfg["wq"]), [HD // 128, 128, 16, 128])
    wk = ctx.input(name + "wk", lambda c: prep_w(cfg["wk"]), [HD // 128, 128, 16, 128])
    wv = ctx.input(name + "wv", lambda c: prep_w(cfg["wv"]), [HD // 128, 128, 16, 128])
    has_rope = cfg.get("qm") is not None
    if has_rope:
        wqr = ctx.input(name + "wqr", lambda c: prep_w(cfg["wqr"]), [16, 128, 16, 128])
        wkr = ctx.input(name + "wkr", lambda c: prep_w(cfg["wkr"]), [16, 128, 16, 128])
        qm = ctx.input(name + "qm", cfg["qm"], [HD, T])
        qm2 = ctx.input(name + "qm2", cfg["qm2"], [HD, T])
        km = ctx.input(name + "km", cfg["km"], [HD, T])
        km2 = ctx.input(name + "km2", cfg["km2"], [HD, T])
    if cfg.get("bq") is not None:
        def bpack(c):
            a = np.concatenate([cfg["bq"].reshape(16, 128).T,
                                cfg["bk"].reshape(16, 128).T,
                                cfg["bv"].reshape(16, 128).T], axis=1)
            return np.ascontiguousarray(a).astype(np.float32)
        bqkv = ctx.input(name + "bqkv", bpack, [128, 48])
        bias_sb = ctx.sbuf(name + "bias", [128, 48])
        btk = ctx.dma("sy", bias_sb[:], bqkv[:, :])
    else:
        bias_sb, btk = None, None

    def decay_pack(c):
        i = np.arange(CH, dtype=np.float64)
        out = np.zeros((nh, 128, 260), dtype=np.float64)
        for h in range(nh):
            g = float(gam[h])
            Dm = np.where(i[:, None] >= i[None, :],
                          np.exp(np.log(g) * (i[:, None] - i[None, :])), 0.0)
            out[h, :, :128] = Dm.T
            out[h, :, 128] = np.exp(np.log(g) * (CH - 1.0 - i))
            out[h, :, 129] = g ** CH
            out[h, :, 130:258] = (np.exp(np.log(g) * (i + 1.0)))[None, :]
        return np.ascontiguousarray(out.transpose(1, 0, 2).reshape(
            128, nh * 260)).astype(np.float32)
    dk = ctx.input(name + "dk", decay_pack, [128, nh * 260])
    dk_sb = ctx.sbuf("dksb", [128, 520])

    def comb_pack(c):
        q = c % 4
        out = np.zeros((nh, 128, 4), dtype=np.float64)
        for h in range(nh):
            g = float(gam[h])
            for cp in range(4):
                if cp < q:
                    out[h, :, cp] = np.exp(np.log(g) * (T * (q - 1 - cp)))
        return np.ascontiguousarray(out.transpose(1, 0, 2).reshape(
            128, nh * 4)).astype(np.float32)
    cmb = ctx.input(name + "cmb", comb_pack, [128, nh * 4])
    cmb_sb = ctx.sbuf("cmbsb", [128, 64])
    ctk = ctx.dma("sy", cmb_sb[:, 0:nh * 4], cmb[:, :])

    rg = DEV_RG if DEV_RG is not None else [[0, 1, 2, 3], [4, 5, 6, 7]]
    nrg = len(rg[0])
    xn_tk = emit_ln(ctx, sb, resid_dram)
    if cfg.get("dump") == "xn":
        for c in range(NCH):
            ctx.dma("gp", qT[c * 128:(c + 1) * 128, :], X[c][:], waits=[xn_tk[-1]])
        ctx.barrier()
        return qT
    cr = None
    if cfg.get("post") == "l0":
        wg = ctx.input(name + "wg", lambda c: prep_w(cfg["wg"]), [16, 128, 16, 128])
        sgT = ctx.dram(name + "_sgT", [D, T])
        cr = emit_proj_slices(ctx, sb, wg, sgT, xn_tk, silu=True, carry=cr)

    cr = emit_proj_slices(ctx, sb, wq, qT, xn_tk,
                     extra_tbl=(wqr, qm, qm2) if has_rope else None,
                     bias=bias_sb if bias_sb is not None else None, carry=cr)
    cr = emit_proj_slices(ctx, sb, wk, kT, xn_tk,
                     extra_tbl=(wkr, km, km2) if has_rope else None,
                     bias=(bias_sb[:, 16:32] if bias_sb is not None else None),
                     carry=cr)
    cr = emit_proj_slices(ctx, sb, wv, vT, xn_tk,
                     bias=(bias_sb[:, 32:48] if bias_sb is not None else None),
                     carry=cr)
    ctx.barrier()
    if cfg.get("dump"):
        return qT

    prev_head = None
    joins1 = {}
    prev_store = None
    prev_state = None
    kh = X[0:ndc]
    vh = X[ndc:2 * ndc]
    Sst = [X[4 + j][:, 0:dh] for j in range(ndc)]
    kt_t = X[6][:, 0:dh]
    vt_t = X[7][:, 0:dh]
    for h in range(nh):
        base = (h % 2) * 260
        dkld = ctx.dma("sy", dk_sb[:, base:base + 260],
                       dk[:, h * 260:(h + 1) * 260], waits=[joins1.get(h - 2)])
        ld = [ctx.dma("sy", kh[j][:], kT[h * dh + j * 128: h * dh + (j + 1) * 128, :],
                      waits=[prev_head]) for j in range(ndc)]
        ld += [ctx.dma("sy", vh[j][:], vT[h * dh + j * 128: h * dh + (j + 1) * 128, :],
                       waits=[prev_head]) for j in range(ndc)]
        kvs = dk_sb[:, base + 128: base + 129]
        cdk = dk_sb[:, base + 129: base + 130]
        zs = [ctx.op("ve", lambda eng, SS, j=j: eng.memset(Sst[j], 0.0),
                     waits=[prev_head]) for j in range(ndc)]
        for c in range(NCHK):
            tks = []
            for j in range(ndc):
                ctx.transpose(j, kh[j][:, c * 128:(c + 1) * 128], ident[:],
                              waits=ld)
                tks.append(ctx.evac(j, "ve", lambda eng, SS, pt, j=j, kvs=kvs:
                                    eng.tensor_scalar(
                                        out=kt_t[:, j * 128:(j + 1) * 128],
                                        in0=pt[:, :128], scalar1=kvs, scalar2=None,
                                        op0=ALU.mult),
                                    waits=[dkld, prev_store, prev_state]))
                ctx.transpose(2 + j, vh[j][:, c * 128:(c + 1) * 128], ident[:],
                              waits=ld)
                tks.append(ctx.evac(2 + j, "ve", lambda eng, SS, pt, j=j:
                                    eng.tensor_copy(out=vt_t[:, j * 128:(j + 1) * 128],
                                                    in_=pt[:, :128]),
                                    waits=[prev_store, prev_state]))
            st1 = ctx.dma("gp", ktok[c * 128:(c + 1) * 128, h * dh:(h + 1) * dh],
                          kt_t, waits=tks)
            prev_store = ctx.dma("gp", vtok[c * 128:(c + 1) * 128, h * dh:(h + 1) * dh],
                                 vt_t, waits=tks)
            for j in range(ndc):
                bi = 4 + j
                prev_state = ctx.mm(bi, kt_t[:, j * 128:(j + 1) * 128], vt_t[:, 0:dh],
                                    start=True, stop=True, waits=tks, n=dh,
                                    extra_inc="gdone")
                ctx.evac(bi, "ve", lambda eng, SS, pt, j=j, cdk=cdk:
                    eng.scalar_tensor_tensor(
                        out=Sst[j], in0=Sst[j], scalar=cdk, op0=ALU.mult,
                        in1=pt[:, :dh], op1=ALU.add), waits=[zs[j]])
        sttk = [ctx.dma("gp", stp[h * dh + j * 128: h * dh + (j + 1) * 128, :],
                        Sst[j], waits=[("ev", ctx.cnt["ev"])])
                for j in range(ndc)]
        prev_head = sttk[-1]
        joins1[h] = sttk[-1]
    ctx.barrier()
    ag = ctx.op("gp", lambda eng, SS: eng.collective_compute(
        "AllGather", ALU.bypass, replica_groups=rg,
        ins=[stp.tensor.ap().opt()],
        outs=[stg.tensor.ap()[0:nrg * HD, :].opt()]), inc="cc")
    ctx.raw("gp", lambda eng, SS, _v=ctx.cnt["cc"]: eng.wait_ge(SS["cc"], _v))
    ctx.barrier()

    prev_head = None
    joins2 = {}
    qh = X[0:ndc]
    kh2 = X[2:2 + ndc]
    Sst = [X[4 + j][:, 0:dh] for j in range(ndc)]
    sld = sb["A0"]
    kt_t = X[6][:, 0:dh]
    vt_t = X[7][:, 0:dh]
    qcs = X[8][:, 0:dh]
    ptile = X[9][:, 0:128]
    oT = X[10][:, 0:dh]
    prev_y = None
    prev_av = None
    prev_state = None
    state_ev = None
    for h in range(nh):
        base = (h % 2) * 260
        dkld = ctx.dma("sy", dk_sb[:, base:base + 260],
                       dk[:, h * 260:(h + 1) * 260], waits=[joins2.get(h - 2)])
        ld = [ctx.dma("sy", qh[j][:], qT[h * dh + j * 128: h * dh + (j + 1) * 128, :],
                      waits=[prev_head]) for j in range(ndc)]
        ld += [ctx.dma("sy", kh2[j][:], kT[h * dh + j * 128: h * dh + (j + 1) * 128, :],
                       waits=[prev_head]) for j in range(ndc)]
        zs = []
        for j in range(ndc):
            z = ctx.op("ve", lambda eng, SS, j=j: eng.memset(Sst[j], 0.0),
                       waits=[ctk, prev_head])
            for cp in range(nrg):
                lt = ctx.dma("sy", sld[:, 0:dh],
                             stg[cp * HD + h * dh + j * 128:
                                 cp * HD + h * dh + (j + 1) * 128, :], waits=[z])
                z = ctx.op("ve", lambda eng, SS, j=j, cp=cp, h=h:
                    eng.scalar_tensor_tensor(
                        out=Sst[j], in0=sld[:, 0:dh],
                        scalar=cmb_sb[:, h * 4 + cp: h * 4 + cp + 1],
                        op0=ALU.mult, in1=Sst[j], op1=ALU.add), waits=[lt])
            zs.append(z)
        cdk = dk_sb[:, base + 129: base + 130]
        dmask = dk_sb[:, base: base + 128]
        csb = dk_sb[:, base + 130: base + 258]
        for c in range(NCHK):
            cs = slice(c * 128, (c + 1) * 128)
            for j in range(ndc):
                ctx.mm(0, kh2[j][:, cs], qh[j][:, cs], start=(j == 0),
                       stop=(j == ndc - 1),
                       waits=(ld + [prev_head]) if j == 0 else None, n=128)
            pt_tk = ctx.evac(0, "ve", lambda eng, SS, pt, dmask=dmask:
                eng.tensor_tensor(
                    out=ptile[:, 0:128], in0=pt[:, :128], in1=dmask, op=ALU.mult),
                waits=[dkld, prev_av])
            l1 = ctx.dma("sy", kt_t, ktok[cs, h * dh:(h + 1) * dh],
                         waits=[prev_state])
            l2 = ctx.dma("sy", vt_t, vtok[cs, h * dh:(h + 1) * dh],
                         waits=[prev_state, prev_av])
            qcs_tk = []
            for j in range(ndc):
                qcs_tk.append(ctx.op("ve", lambda eng, SS, j=j, cs=cs, csb=csb:
                                     eng.tensor_tensor(
                                         out=qcs[:, j * 128:(j + 1) * 128],
                                         in0=qh[j][:, cs], in1=csb, op=ALU.mult),
                                     waits=[dkld, prev_av]))
            yevs = []
            for vj in range(ndc):
                bi = 2 + vj
                for j in range(ndc):
                    ctx.mm(bi, Sst[j][:, vj * 128:(vj + 1) * 128],
                           qcs[:, j * 128:(j + 1) * 128],
                           start=(j == 0), stop=False,
                           waits=[qcs_tk[j], zs[j], state_ev], n=128)
                prev_av = ctx.mm(bi, vt_t[:, vj * 128:(vj + 1) * 128],
                                 ptile[:, 0:128], start=False, stop=True,
                                 waits=[pt_tk, l2], n=128,
                                 extra_inc=("gdone" if vj == ndc - 1 else None))
                yevs.append(ctx.evac(bi, "sc", lambda eng, SS, pt, vj=vj:
                    eng.activation(
                        out=oT[:, vj * 128:(vj + 1) * 128], in_=pt[:, :128],
                        func=ACT.Copy), waits=[prev_y]))
            for vj in range(ndc):
                prev_y = ctx.dma("gp", yT[h * dh + vj * 128: h * dh + (vj + 1) * 128, cs],
                                 oT[:, vj * 128:(vj + 1) * 128], waits=yevs)
            for j in range(ndc):
                bi = 6 + (j % 2)
                prev_state = ctx.mm(bi, kt_t[:, j * 128:(j + 1) * 128], vt_t[:, 0:dh],
                                    start=True, stop=True, waits=[l1, l2], n=dh,
                                    extra_inc="gdone")
                state_ev = ctx.evac(bi, "ve", lambda eng, SS, pt, j=j, cdk=cdk:
                    eng.scalar_tensor_tensor(
                        out=Sst[j], in0=Sst[j], scalar=cdk, op0=ALU.mult,
                        in1=pt[:, :dh], op1=ALU.add),
                    waits=[("mm", ctx.cnt["mm"])])
        prev_head = ctx.op("ve", lambda eng, SS: eng.engine_nop(),
                           waits=[("ev", ctx.cnt["ev"]), ("mm", ctx.cnt["mm"]),
                                  prev_y])
        joins2[h] = prev_head
    ctx.barrier()
    return yT



def emit_l0_ffn(ctx, sb, resid_dram, core, host, dbg=None):
    X = sb["X"]
    A0, A1, A2, A3, A4 = sb["A0"], sb["A1"], sb["A2"], sb["A3"], sb["A4"]
    W0, W1 = sb["W0"], sb["W1"]
    ones = sb["ones128"]
    ident = sb["ident"]
    sm0, sm1 = sb["sm0"], sb["sm1"]
    NE = 18

    def up_pack(c):
        ws = [host["L0_ffn_Wsh_up"][i] for i in range(2)] + \
             [host["L0_ffn_Wrt_up"][i] for i in range(16)]
        return np.stack([prep_w(w.astype(np.float64)) for w in ws])

    def dn_pack(c):
        ws = [host["L0_ffn_Wsh_down"][i] for i in range(2)] + \
             [host["L0_ffn_Wrt_down"][i] for i in range(16)]
        return np.stack([prep_w(w.astype(np.float64)) for w in ws])
    upk = ctx.input("f0up", up_pack, [NE, 8, 128, 16, 128])
    dnk = ctx.input("f0dn", dn_pack, [NE, 16, 128, 8, 128])

    def r_pack(c):
        Wr = host["L0_ffn_W_router"].astype(np.float32)
        return np.ascontiguousarray(
            Wr.reshape(16, 128, 16).transpose(1, 0, 2).reshape(128, 256))
    rpk = ctx.input("f0r", r_pack, [128, 256])
    mid_d = ctx.dram("f0mid", [NE * 1024, T])

    xn_tk = emit_ln(ctx, sb, resid_dram)
    rld = ctx.dma("sy", W0[:, 0:256], rpk[:, :], waits=[xn_tk[-1]])
    revs = []
    last_rmm = None
    for tt in range(NTT):
        for dc in range(NCH):
            last_rmm = ctx.mm(tt, W0[:, dc * 16:(dc + 1) * 16],
                              X[dc][:, tt * 512:(tt + 1) * 512],
                              start=(dc == 0), stop=(dc == NCH - 1),
                              waits=[rld] if dc == 0 else None, op=16)
        revs.append(ctx.evac(tt, "ve", lambda eng, SS, pt, tt=tt:
                    eng.tensor_copy(out=A4[0:16, tt * 512:(tt + 1) * 512],
                                    in_=pt[0:16, :512])))
    epsb, epstk = ctx.constf(EPS)
    gtks = []
    for c in range(16):
        cs = slice(c * 128, (c + 1) * 128)
        bi = 4 + c % 4
        ctx.transpose(bi, A4[0:16, cs], ident[0:16, 0:16], waits=revs, n=16)
        base = (c % 4) * 64
        w = sm0[:, base:base + 16]
        w2 = sm0[:, base + 16:base + 32]
        w3 = sm0[:, base + 32:base + 48]
        sc0 = sm0[:, base + 48:base + 49]
        sc1 = sm0[:, base + 49:base + 50]
        t = ctx.evac(bi, "ve", lambda eng, SS, pt, w=w: eng.tensor_copy(
            out=w, in_=pt[:, :16]), waits=[gtks[c - 4] if c >= 4 else None])
        t = ctx.op("ve", lambda eng, SS, w=w, sc0=sc0: eng.tensor_reduce(
            out=sc0, in_=w, axis=AX.X, op=ALU.max), waits=[t])
        t = ctx.op("ve", lambda eng, SS, sc0=sc0, sc1=sc1: eng.tensor_scalar(
            out=sc1, in0=sc0, scalar1=-1.0, scalar2=None, op0=ALU.mult),
            waits=[t])
        t = ctx.op("sc", lambda eng, SS, w=w, w2=w2, sc1=sc1: eng.activation(
            out=w2, in_=w, func=ACT.Exp, bias=sc1[:, 0:1]), waits=[t])
        t = ctx.op("ve", lambda eng, SS, w2=w2, sc0=sc0: eng.tensor_reduce(
            out=sc0, in_=w2, axis=AX.X, op=ALU.add), waits=[t])
        t = ctx.op("ve", lambda eng, SS, sc0=sc0, sc1=sc1: eng.reciprocal(
            out=sc1, in_=sc0), waits=[t])
        t = ctx.op("ve", lambda eng, SS, w2=w2, sc1=sc1: eng.tensor_scalar(
            out=w2, in0=w2, scalar1=sc1[:, 0:1], scalar2=None, op0=ALU.mult),
            waits=[t])
        t = ctx.op("ve", lambda eng, SS, w2=w2, sc0=sc0: eng.tensor_reduce(
            out=sc0, in_=w2, axis=AX.X, op=ALU.max), waits=[t])
        t = ctx.op("ve", lambda eng, SS, w2=w2, w3=w3, sc0=sc0:
            eng.tensor_scalar(out=w3, in0=w2, scalar1=sc0[:, 0:1],
                              scalar2=-1e30, op0=ALU.is_ge, op1=ALU.mult),
            waits=[t])
        t = ctx.op("ve", lambda eng, SS, w2=w2, w3=w3: eng.tensor_tensor(
            out=w3, in0=w2, in1=w3, op=ALU.add), waits=[t])
        t = ctx.op("ve", lambda eng, SS, w3=w3, sc0=sc0: eng.tensor_reduce(
            out=sc0, in_=w3, axis=AX.X, op=ALU.max), waits=[t])
        t = ctx.op("ve", lambda eng, SS, w2=w2, w3=w3, sc0=sc0:
            eng.tensor_scalar(out=w3, in0=w2, scalar1=sc0[:, 0:1],
                              scalar2=None, op0=ALU.is_ge), waits=[t])
        t = ctx.op("ve", lambda eng, SS, w2=w2, w3=w3, c=c: eng.tensor_tensor(
            out=sm1[:, c * 16:(c + 1) * 16], in0=w2, in1=w3, op=ALU.mult),
            waits=[t])
        bi2 = c % 4
        ctx.transpose(bi2, sm1[:, c * 16:(c + 1) * 16], ident[:], waits=[t],
                      op=16)
        gtks.append(ctx.evac(bi2, "ve", lambda eng, SS, pt, cs=cs:
                    eng.tensor_copy(out=A2[0:16, cs], in_=pt[0:16, :128])))
    if dbg == "gates":
        for c in range(NCH):
            t1 = ctx.op("ve", lambda eng, SS, c=c: eng.memset(A0[:], 0.0),
                        waits=gtks)
            t1 = ctx.op("ve", lambda eng, SS, c=c: eng.tensor_copy(
                out=A0[0:16, :], in_=A2[0:16, :]), waits=[t1])
            ctx.dma("gp", resid_dram[c * 128:(c + 1) * 128, :], A0[:],
                    waits=[t1])
            ctx.barrier()
        return
    prev_mm = [last_rmm, None]
    prev_st = {}
    prev_gb = None
    for e in range(NE):
        routed = e >= 2
        if routed:
            rr = ctx.dma("sy", A3[0:1, 0:2048], A2[e - 2:e - 1, 0:2048],
                         waits=[gtks[-1], prev_gb])
            gbevs = []
            for tt in range(NTT):
                sl = slice(tt * 512, (tt + 1) * 512)
                ctx.mm(tt, ones[0:1, 0:128], A3[0:1, sl], start=True,
                       stop=True, waits=[rr], n=512)
                gbevs.append(ctx.evac(tt, "ve", lambda eng, SS, pt, sl=sl:
                            eng.tensor_copy(out=A4[:, sl], in_=pt[:, :512]),
                            waits=[prev_gb, gtks[-1]]))
        for s in range(8):
            wt = [W0, W1][s % 2]
            st = [A0, A1][s % 2]
            w = [xn_tk[-1]]
            if prev_mm[s % 2] is not None:
                w.append(prev_mm[s % 2])
            ldtk = ctx.dma("sy", wt[:, :2048],
                           upk[e][s].rearrange("a b c -> a (b c)"), waits=w)
            evs = []
            stg = (prev_st.get((e, s - 2)) if s >= 2
                   else prev_st.get((e - 1, s + 6)))
            for tt in range(NTT):
                bi = (s % 2) * 4 + tt
                sl = slice(tt * 512, (tt + 1) * 512)
                for dc in range(NCH):
                    last = ctx.mm(bi, wt[:, dc * 128:(dc + 1) * 128],
                                  X[dc][:, sl], start=(dc == 0),
                                  stop=(dc == NCH - 1),
                                  waits=[ldtk] if dc == 0 else None)
                evs.append(ctx.evac(bi, "sc", lambda eng, SS, pt, sl=sl, st=st:
                           eng.activation(out=st[:, sl], in_=pt[:, :512],
                                          func=ACT.Gelu),
                           waits=[stg]))
            prev_mm[s % 2] = last
            if routed:
                gm = ctx.op("ve", lambda eng, SS, st=st: eng.tensor_tensor(
                    out=st[:], in0=st[:], in1=A4[:], op=ALU.mult),
                    waits=evs + gbevs)
                prev_gb = gm
                evs = [gm]
            prev_st[(e, s)] = ctx.dma(
                "gp", mid_d[e * 1024 + s * 128: e * 1024 + (s + 1) * 128, :],
                st[:], waits=evs)
    ctx.barrier()
    if dbg == "mid":
        for c in range(NCH):
            t1 = ctx.dma("sy", A0[:], mid_d[c * 128:(c + 1) * 128, :])
            ctx.dma("gp", resid_dram[c * 128:(c + 1) * 128, :], A0[:],
                    waits=[t1])
            ctx.barrier()
        return
    prev_mm = [None, None]
    prev_acc = {}
    prev_ld = None
    for e in range(NE):
        mld = [ctx.dma("sy", X[j][:],
                       mid_d[e * 1024 + j * 128: e * 1024 + (j + 1) * 128, :],
                       waits=[prev_ld]) for j in range(8)]
        for s in range(16):
            wt = [W0, W1][s % 2]
            st = [A0, A1][s % 2]
            w = list(mld)
            if prev_mm[s % 2] is not None:
                w.append(prev_mm[s % 2])
            ldtk = ctx.dma("sy", wt[:, 0:1024],
                           dnk[e][s].rearrange("a b c -> a (b c)"), waits=w)
            evs = []
            stg = (prev_acc.get((e, s - 2)) if s >= 2
                   else prev_acc.get((e - 1, s + 14)))
            for tt in range(NTT):
                bi = (s % 2) * 4 + tt
                sl = slice(tt * 512, (tt + 1) * 512)
                for dc in range(8):
                    last = ctx.mm(bi, wt[:, dc * 128:(dc + 1) * 128],
                                  X[dc][:, sl], start=(dc == 0),
                                  stop=(dc == 7),
                                  waits=[ldtk] if dc == 0 else None)
                evs.append(ctx.evac(bi, "ve", lambda eng, SS, pt, sl=sl, st=st:
                           eng.tensor_scalar(out=st[:, sl], in0=pt[:, :512],
                                             scalar1=0.5, scalar2=None,
                                             op0=ALU.mult),
                           waits=[stg]))
            prev_mm[s % 2] = last
            if s == 15:
                prev_ld = last
            prev_acc[(e, s)] = ctx.dma(
                "gp", resid_dram[s * 128:(s + 1) * 128, :], st[:],
                waits=evs + [prev_acc.get((e - 1, s))], accum=True)
    ctx.barrier()


def emit_l3_ffn(ctx, sb, resid_dram, core, host, dbg=None):
    X = sb["X"]
    A0, A1, A2, A3, A4 = sb["A0"], sb["A1"], sb["A2"], sb["A3"], sb["A4"]
    W0, W1 = sb["W0"], sb["W1"]
    ones = sb["ones128"]
    ident = sb["ident"]
    sm0, sm1 = sb["sm0"], sb["sm1"]
    NA = 16384
    NKC = NA // 128
    mask_idx = np.asarray(host["L3_ffn_mask_idx"]).astype(np.int64)
    rows = mask_idx // D
    cols = mask_idx % D
    order = np.argsort(rows, kind="stable")
    rs, csr = rows[order], cols[order]

    def cpk_fn(c):
        out = np.zeros((NKC, 128, 2048), dtype=np.float32)
        for kc in range(NKC):
            for kl in range(128):
                cc = csr[kc * 128 + kl]
                out[kc, cc % 128, (cc // 128) * 128 + kl] = 1.0
        return out
    cpk = ctx.input("f3cpk", cpk_fn, [NKC, 128, 2048])
    pairs = []
    for s in range(16):
        ks = np.nonzero((rs >= s * 128) & (rs < (s + 1) * 128))[0]
        kcs = sorted(set(int(k) // 128 for k in ks))
        for kc in kcs:
            pairs.append((s, kc))
    chunks_of = {s: [kc for (s2, kc) in pairs if s2 == s] for s in range(16)}

    def rlp_fn(c):
        out = np.zeros((len(pairs), 128, 128), dtype=np.float32)
        for pi, (s, kc) in enumerate(pairs):
            for kl in range(128):
                r = rs[kc * 128 + kl]
                if s * 128 <= r < (s + 1) * 128:
                    out[pi, kl, r - s * 128] = 1.0
        return out
    rlp = ctx.input("f3rlp", rlp_fn, [len(pairs), 128, 128])

    def atoms_fn(c):
        return np.ascontiguousarray(
            host["L3_ffn_atoms"].astype(np.float32)[:, order])
    atk = ctx.input("f3atoms", atoms_fn, [64, NA])

    def imp_fn(c):
        return np.ascontiguousarray(
            host["L3_ffn_importance"].astype(np.float32)[:, order])
    imk = ctx.input("f3imp", imp_fn, [8, NA])
    eawk = ctx.input("f3eaw", lambda c: np.ascontiguousarray(
        host["L3_ffn_expert_atom_weights"].astype(np.float32)), [8, 64])

    def gw_fn(c):
        return np.ascontiguousarray(host["L3_ffn_gate_w"].astype(np.float32)
                                    .T.reshape(16, 128, 8).transpose(1, 0, 2)
                                    .reshape(128, 128))
    gwk = ctx.input("f3gw", gw_fn, [128, 128])
    wpk = ctx.input("f3w", lambda c: prep_w(
        host["L3_ffn_W"].astype(np.float64).T.copy()), [16, 128, 16, 128])

    xg_d = ctx.dram("f3xg", [NA, T])
    wgtT_d = ctx.dram("f3wgtT", [NA, 8])
    g_d = ctx.dram("f3g", [8, T])

    xn_tk = emit_ln(ctx, sb, resid_dram)

    ew = sm0[:, 0:64]
    ld_e = ctx.dma("sy", ew[0:8, :], eawk[:, :])
    t = ctx.op("ve", lambda eng, SS: eng.tensor_reduce(
        out=sm0[0:8, 64:65], in_=ew[0:8, 0:64], axis=AX.X, op=ALU.max),
        waits=[ld_e])
    t = ctx.op("ve", lambda eng, SS: eng.tensor_scalar(
        out=sm0[0:8, 65:66], in0=sm0[0:8, 64:65], scalar1=-1.0, scalar2=None,
        op0=ALU.mult), waits=[t])
    t = ctx.op("sc", lambda eng, SS: eng.activation(
        out=ew[0:8, :], in_=ew[0:8, :], func=ACT.Exp,
        bias=sm0[0:8, 65:66]), waits=[t])
    t = ctx.op("ve", lambda eng, SS: eng.tensor_reduce(
        out=sm0[0:8, 64:65], in_=ew[0:8, 0:64], axis=AX.X, op=ALU.add),
        waits=[t])
    t = ctx.op("ve", lambda eng, SS: eng.reciprocal(
        out=sm0[0:8, 65:66], in_=sm0[0:8, 64:65]), waits=[t])
    t = ctx.op("ve", lambda eng, SS: eng.tensor_scalar(
        out=ew[0:8, :], in0=ew[0:8, :], scalar1=sm0[0:8, 65:66],
        scalar2=None, op0=ALU.mult), waits=[t])
    ctx.transpose(0, ew[0:8, 0:64], ident[0:8, 0:8], waits=[t], n=8, op=64)
    smT_tk = ctx.evac(0, "ve", lambda eng, SS, pt: eng.tensor_copy(
        out=sm1[0:64, 0:8], in_=pt[0:64, :8]))
    prev_w = None
    prev_wsl = {}
    prev_mul = None
    for ac in range(NA // 2048):
        asl = slice(ac * 2048, (ac + 1) * 2048)
        lda = ctx.dma("sy", W1[0:64, :], atk[:, asl], waits=[prev_w])
        ldi = ctx.dma("sy", A4[0:8, 0:2048], imk[:, asl], waits=[prev_w])
        sg = ctx.op("sc", lambda eng, SS: eng.activation(
            out=A0[0:8, 0:2048], in_=A4[0:8, 0:2048], func=ACT.Sigmoid),
            waits=[ldi])
        for tt in range(NTT):
            sl = slice(tt * 512, (tt + 1) * 512)
            ctx.mm(tt, sm1[0:64, 0:8], W1[0:64, sl], start=True, stop=True,
                   waits=[lda, smT_tk], op=8)
            prev_mul = ctx.evac(tt, "ve", lambda eng, SS, pt, sl=sl:
                eng.tensor_tensor(
                    out=A4[0:8, sl], in0=pt[0:8, :512], in1=A0[0:8, sl],
                    op=ALU.mult), waits=[sg])
            for q in range(4):
                qs = slice(tt * 512 + q * 128, tt * 512 + (q + 1) * 128)
                bi = 4 + q
                col = (tt * 4 + q) * 8
                ctx.transpose(bi, A4[0:8, qs], ident[0:8, 0:8],
                              waits=[prev_mul], n=8, op=128)
                ev = ctx.evac(bi, "ve", lambda eng, SS, pt, col=col:
                    eng.tensor_copy(out=A3[:, col:col + 8], in_=pt[:, :8]),
                    waits=[prev_wsl.get(col)])
                prev_w = ctx.dma("gp",
                                 wgtT_d[ac * 2048 + tt * 512 + q * 128:
                                        ac * 2048 + tt * 512 + (q + 1) * 128, :],
                                 A3[:, col:col + 8], waits=[ev])
                prev_wsl[col] = prev_w
    gld = ctx.dma("sy", W0[:, 0:128], gwk[:, :], waits=[xn_tk[-1], prev_w])
    revs = []
    last_rmm = None
    for tt in range(NTT):
        for dc in range(NCH):
            last_rmm = ctx.mm(tt, W0[:, dc * 8:(dc + 1) * 8],
                              X[dc][:, tt * 512:(tt + 1) * 512],
                              start=(dc == 0), stop=(dc == NCH - 1),
                              waits=[gld] if dc == 0 else None, op=8)
        revs.append(ctx.evac(tt, "ve", lambda eng, SS, pt, tt=tt:
                    eng.tensor_copy(out=A4[0:8, tt * 512:(tt + 1) * 512],
                                    in_=pt[0:8, :512])))
    gstores = []
    for c in range(16):
        cs = slice(c * 128, (c + 1) * 128)
        bi = 4 + c % 4
        ctx.transpose(bi, A4[0:8, cs], ident[0:8, 0:8], waits=revs, n=8)
        base = (c % 4) * 64
        w = sm0[:, base:base + 8]
        w2 = sm0[:, base + 8:base + 16]
        w3 = sm0[:, base + 16:base + 24]
        sc0 = sm0[:, base + 24:base + 25]
        sc1 = sm0[:, base + 25:base + 26]
        t = ctx.evac(bi, "ve", lambda eng, SS, pt, w=w: eng.tensor_copy(
            out=w, in_=pt[:, :8]),
            waits=[gstores[c - 4] if c >= 4 else None])
        t = ctx.op("ve", lambda eng, SS, w=w: eng.tensor_single_scalar(
            out=w, in_=w, scalar=50.0, op=ALU.min), waits=[t])
        t = ctx.op("ve", lambda eng, SS, w=w: eng.tensor_single_scalar(
            out=w, in_=w, scalar=-50.0, op=ALU.max), waits=[t])
        t = ctx.op("ve", lambda eng, SS, w=w, sc0=sc0: eng.tensor_reduce(
            out=sc0, in_=w, axis=AX.X, op=ALU.max), waits=[t])
        t = ctx.op("ve", lambda eng, SS, w=w, w3=w3, sc0=sc0:
            eng.tensor_scalar(out=w3, in0=w, scalar1=sc0[:, 0:1],
                              scalar2=-1e30, op0=ALU.is_ge, op1=ALU.mult),
            waits=[t])
        t = ctx.op("ve", lambda eng, SS, w=w, w3=w3: eng.tensor_tensor(
            out=w3, in0=w, in1=w3, op=ALU.add), waits=[t])
        t = ctx.op("ve", lambda eng, SS, w3=w3, sc1=sc1: eng.tensor_reduce(
            out=sc1, in_=w3, axis=AX.X, op=ALU.max), waits=[t])
        t = ctx.op("ve", lambda eng, SS, w=w, w3=w3, sc1=sc1:
            eng.tensor_scalar(out=w3, in0=w, scalar1=sc1[:, 0:1],
                              scalar2=None, op0=ALU.is_ge), waits=[t])
        t2 = ctx.op("ve", lambda eng, SS, sc0=sc0, sc1=sc1: eng.tensor_scalar(
            out=sc1, in0=sc0, scalar1=-1.0, scalar2=None, op0=ALU.mult),
            waits=[t])
        t = ctx.op("sc", lambda eng, SS, w=w, w2=w2, sc1=sc1: eng.activation(
            out=w2, in_=w, func=ACT.Exp, bias=sc1[:, 0:1]), waits=[t2])
        t = ctx.op("ve", lambda eng, SS, w2=w2, w3=w3: eng.tensor_tensor(
            out=w2, in0=w2, in1=w3, op=ALU.mult), waits=[t])
        t = ctx.op("ve", lambda eng, SS, w2=w2, sc0=sc0: eng.tensor_reduce(
            out=sc0, in_=w2, axis=AX.X, op=ALU.add), waits=[t])
        t = ctx.op("ve", lambda eng, SS, sc0=sc0, sc1=sc1: eng.reciprocal(
            out=sc1, in_=sc0), waits=[t])
        t = ctx.op("ve", lambda eng, SS, w2=w2, sc1=sc1: eng.tensor_scalar(
            out=w2, in0=w2, scalar1=sc1[:, 0:1], scalar2=None, op0=ALU.mult),
            waits=[t])
        bi2 = c % 4
        ctx.transpose(bi2, w2, ident[:], waits=[t], op=8)
        gv = ctx.evac(bi2, "ve", lambda eng, SS, pt: eng.tensor_copy(
            out=A3[0:8, 0:128], in_=pt[0:8, :128]))
        gstores.append(ctx.dma("gp", g_d[:, cs], A3[0:8, 0:128], waits=[gv]))
    if dbg == "gates":
        for c in range(NCH):
            t1 = ctx.op("ve", lambda eng, SS: eng.memset(A0[:], 0.0),
                        waits=gstores)
            t1 = ctx.dma("sy", A0[0:8, :], g_d[:, :], waits=[t1])
            ctx.dma("gp", resid_dram[c * 128:(c + 1) * 128, :], A0[:],
                    waits=[t1])
            ctx.barrier()
        return
    if dbg == "wgt":
        for c in range(NCH):
            t1 = ctx.dma("sy", A0[:, 0:8], wgtT_d[c * 128:(c + 1) * 128, :])
            t2 = ctx.op("ve", lambda eng, SS: eng.memset(A1[:], 0.0),
                        waits=[t1])
            t2 = ctx.op("ve", lambda eng, SS: eng.tensor_copy(
                out=A1[:, 0:8], in_=A0[:, 0:8]), waits=[t2])
            ctx.dma("gp", resid_dram[c * 128:(c + 1) * 128, :], A1[:],
                    waits=[t2])
            ctx.barrier()
        return
    cb, _ = resid_evac_maker(ctx, sb, resid_dram, 0.5)
    emit_proj(ctx, sb, wpk, D, D, X, [xn_tk[-1], ("mm", ctx.cnt["mm"])], cb)
    ctx.barrier()
    prev_mm = [None, None]
    prev_st = {}
    for kc in range(NKC):
        wt = [W0, W1][kc % 2]
        st = [A0, A1][kc % 2]
        w = [prev_mm[kc % 2]] if prev_mm[kc % 2] else []
        ldtk = ctx.dma("sy", wt[:, :2048], cpk[kc], waits=w)
        evs = []
        for tt in range(NTT):
            bi = (kc % 2) * 4 + tt
            sl = slice(tt * 512, (tt + 1) * 512)
            for ic in range(NCH):
                last = ctx.mm(bi, wt[:, ic * 128:(ic + 1) * 128],
                              X[ic][:, sl], start=(ic == 0),
                              stop=(ic == NCH - 1),
                              waits=[ldtk] if ic == 0 else None)
            evs.append(ctx.evac(bi, "ve", lambda eng, SS, pt, sl=sl, st=st:
                       eng.tensor_copy(out=st[:, sl], in_=pt[:, :512]),
                       waits=[prev_st.get(kc - 2)]))
        prev_mm[kc % 2] = last
        prev_st[kc] = ctx.dma("gp", xg_d[kc * 128:(kc + 1) * 128, :], st[:],
                              waits=evs)
    ctx.barrier()
    if dbg == "xg":
        for c in range(NCH):
            t1 = ctx.dma("sy", A0[:], xg_d[c * 128:(c + 1) * 128, :])
            ctx.dma("gp", resid_dram[c * 128:(c + 1) * 128, :], A0[:],
                    waits=[t1])
            ctx.barrier()
        return
    pi_of = {p: i for i, p in enumerate(pairs)}
    prev_acc = {}
    prev_gld = None
    prev_mm_e = None
    for s in range(16):
        kcs = chunks_of[s]
        assert len(kcs) <= 12, f"slice {s} has {len(kcs)} chunks"
        if not kcs:
            continue
        gl = [ctx.dma("sy", X[i][:],
                      xg_d[kc * 128:(kc + 1) * 128, :],
                      waits=[prev_gld]) for i, kc in enumerate(kcs)]
        wl = [ctx.dma("sy", X[12 + i % 2][:, (i // 2) * 8:(i // 2) * 8 + 8],
                      wgtT_d[kc * 128:(kc + 1) * 128, :],
                      waits=[prev_gld]) for i, kc in enumerate(kcs)]
        rl = [ctx.dma("sy", X[14][:, i * 128:(i + 1) * 128],
                      rlp[pi_of[(s, kc)]], waits=[prev_gld])
              for i, kc in enumerate(kcs)]
        for e in range(8):
            par = e % 2
            grow = ctx.dma("sy", A3[0:1, 0:2048],
                           g_d[e:e + 1, :], waits=[prev_mm_e])
            gbevs = []
            for tt in range(NTT):
                sl = slice(tt * 512, (tt + 1) * 512)
                bi = par * 4 + tt
                ctx.mm(bi, ones[0:1, 0:128], A3[0:1, sl], start=True,
                       stop=True, waits=[grow], n=512)
                gbevs.append(ctx.evac(bi, "ve", lambda eng, SS, pt, sl=sl:
                            eng.tensor_scalar(out=A4[:, sl], in0=pt[:, :512],
                                              scalar1=0.5, scalar2=None,
                                              op0=ALU.mult)))
            sevs = []
            for i, kc in enumerate(kcs):
                sevs.append(ctx.op("ve", lambda eng, SS, i=i, e=e:
                    eng.tensor_scalar(
                        out=A2[:, i * 128:(i + 1) * 128],
                        in0=X[14][:, i * 128:(i + 1) * 128],
                        scalar1=X[12 + i % 2][:, (i // 2) * 8 + e:
                                              (i // 2) * 8 + e + 1],
                        scalar2=None, op0=ALU.mult),
                    waits=[rl[i], wl[i], prev_mm_e]))
            evs = []
            for tt in range(NTT):
                bi = par * 4 + tt
                sl = slice(tt * 512, (tt + 1) * 512)
                for i, kc in enumerate(kcs):
                    last = ctx.mm(bi, A2[:, i * 128:(i + 1) * 128],
                                  X[i][:, sl], start=(i == 0),
                                  stop=(i == len(kcs) - 1),
                                  waits=[sevs[i], gl[i]])
                ev = ctx.evac(bi, "ve", lambda eng, SS, pt, sl=sl:
                              eng.tensor_tensor(out=A0[:, sl], in0=pt[:, :512],
                                                in1=A4[:, sl], op=ALU.mult),
                              waits=gbevs + [prev_acc.get((s - 1, 7))])
                if e > 0:
                    ev = ctx.op("ve", lambda eng, SS, sl=sl: eng.tensor_tensor(
                        out=A1[:, sl], in0=A1[:, sl], in1=A0[:, sl],
                        op=ALU.add), waits=[ev])
                else:
                    ev = ctx.op("ve", lambda eng, SS, sl=sl: eng.tensor_copy(
                        out=A1[:, sl], in_=A0[:, sl]),
                        waits=[ev, prev_acc.get((s - 1, "st"))])
                evs.append(ev)
            prev_acc[(s, e)] = evs[-1]
            prev_mm_e = last
        acc = ctx.dma("gp", resid_dram[s * 128:(s + 1) * 128, :], A1[:],
                      waits=evs, accum=True)
        prev_acc[(s, "st")] = acc
        prev_gld = acc
    ctx.barrier()


def emit_l0_post(ctx, sb, resid_dram, yT, sgT, host, name="a0"):
    X = sb["X"]
    A0, A1, A2, A3 = sb["A0"], sb["A1"], sb["A2"], sb["A3"]
    W0, W1 = sb["W0"], sb["W1"]
    ident = sb["ident"]
    sm0 = sb["sm0"]
    gnwb = ctx.input(name + "gnw", lambda c: np.ascontiguousarray(
        np.broadcast_to(host["L0_attn_gn_weight"].astype(np.float32),
                        (128, D))), [128, D])
    gnbb = ctx.input(name + "gnb", lambda c: np.ascontiguousarray(
        np.broadcast_to(host["L0_attn_gn_bias"].astype(np.float32),
                        (128, D))), [128, D])
    tg = ctx.dma("sy", W0[:], gnwb[:, :])
    tb = ctx.dma("sy", W1[:], gnbb[:, :])
    epsb, epstk = ctx.constf(EPS)
    prev = None
    for c in range(NCH):
        cs = slice(c * 128, (c + 1) * 128)
        lds = [ctx.dma("sy", A0[:, s * 128:(s + 1) * 128],
                       yT[s * 128:(s + 1) * 128, cs], waits=[prev])
               for s in range(16)]
        ldg = [ctx.dma("sy", A3[:, s * 128:(s + 1) * 128],
                       sgT[s * 128:(s + 1) * 128, cs], waits=[prev])
               for s in range(16)]
        ttks = []
        for s in range(16):
            bi = s % 8
            ctx.transpose(bi, A0[:, s * 128:(s + 1) * 128], ident[:],
                          waits=[lds[s]])
            ttks.append(ctx.evac(bi, "ve", lambda eng, SS, pt, s=s:
                        eng.tensor_copy(out=A1[:, s * 128:(s + 1) * 128],
                                        in_=pt[:, :128]), waits=[prev]))
        ntks = []
        for hh in range(16):
            hs = slice(hh * 128, (hh + 1) * 128)
            t1 = ctx.op("ve", lambda eng, SS, hs=hs, hh=hh: eng.tensor_reduce(
                out=sm0[:, hh:hh + 1], in_=A1[:, hs], axis=AX.X, op=ALU.add),
                waits=[ttks[hh]])
            t2 = ctx.op("sc", lambda eng, SS, hs=hs, hh=hh: eng.activation(
                out=A2[:, hs], in_=A1[:, hs], func=ACT.Square,
                accum_out=sm0[:, 16 + hh:17 + hh]), waits=[ttks[hh], prev])
            t3 = ctx.op("ve", lambda eng, SS, hh=hh: eng.tensor_scalar(
                out=sm0[:, 32 + hh:33 + hh], in0=sm0[:, hh:hh + 1],
                scalar1=-1.0 / 128.0, scalar2=None, op0=ALU.mult), waits=[t1])
            t4 = ctx.op("ve", lambda eng, SS, hh=hh: eng.tensor_scalar(
                out=sm0[:, 48 + hh:49 + hh], in0=sm0[:, 32 + hh:33 + hh],
                scalar1=sm0[:, 32 + hh:33 + hh], scalar2=None, op0=ALU.mult),
                waits=[t3])
            t5 = ctx.op("ve", lambda eng, SS, hh=hh: eng.scalar_tensor_tensor(
                out=sm0[:, 48 + hh:49 + hh], in0=sm0[:, 16 + hh:17 + hh],
                scalar=1.0 / 128.0, op0=ALU.mult,
                in1=sm0[:, 48 + hh:49 + hh], op1=ALU.subtract), waits=[t2, t4])
            t6 = ctx.op("sc", lambda eng, SS, hh=hh: eng.activation(
                out=sm0[:, 48 + hh:49 + hh], in_=sm0[:, 48 + hh:49 + hh],
                func=ACT.Ln, bias=epsb[:, 0:1]), waits=[t5, epstk])
            t7 = ctx.op("sc", lambda eng, SS, hh=hh: eng.activation(
                out=sm0[:, 48 + hh:49 + hh], in_=sm0[:, 48 + hh:49 + hh],
                func=ACT.Exp, scale=-0.5), waits=[t6])
            ntks.append(ctx.op("ve", lambda eng, SS, hs=hs, hh=hh:
                eng.tensor_scalar(
                    out=A1[:, hs], in0=A1[:, hs],
                    scalar1=sm0[:, 32 + hh:33 + hh],
                    scalar2=sm0[:, 48 + hh:49 + hh],
                    op0=ALU.add, op1=ALU.mult), waits=[t7]))
        aff = ctx.op("ve", lambda eng, SS: eng.tensor_tensor(
            out=A1[:], in0=A1[:], in1=W0[:], op=ALU.mult), waits=ntks + [tg])
        aff = ctx.op("ve", lambda eng, SS: eng.tensor_tensor(
            out=A1[:], in0=A1[:], in1=W1[:], op=ALU.add), waits=[aff, tb])
        bevs = []
        for s in range(16):
            bi = s % 8
            ctx.transpose(bi, A1[:, s * 128:(s + 1) * 128], ident[:],
                          waits=[aff])
            bevs.append(ctx.evac(bi, "ve", lambda eng, SS, pt, s=s, cs=cs:
                        eng.tensor_tensor(out=X[s][:, cs], in0=pt[:, :128],
                                          in1=A3[:, s * 128:(s + 1) * 128],
                                          op=ALU.mult), waits=[ldg[s]]))
        prev = bevs[-1]
    ctx.barrier()
    wo = ctx.input(name + "wo", lambda c: prep_w(
        host["L0_attn_W_O"].astype(np.float64)), [16, 128, 16, 128])
    cb, _ = resid_evac_maker(ctx, sb, resid_dram, 1.0)
    emit_proj(ctx, sb, wo, D, D, X, [], cb)


def emit_a0(ctx, sb, resid_dram, core, host, dbg=None):
    nh, dh = 16, 128
    gammas = 1.0 - np.exp(np.linspace(np.log(1 / 32.0), np.log(1 / 512.0), nh))
    WQ = host["L0_attn_W_Q"].astype(np.float64)
    WK = host["L0_attn_W_K"].astype(np.float64)
    WV = host["L0_attn_W_V"].astype(np.float64)
    perm = np.concatenate([np.arange(0, dh, 2), np.arange(1, dh, 2)])
    rotp = np.concatenate([np.arange(64, 128), np.arange(0, 64)])

    def packheads(W, p=None, p2=None):
        cols = []
        for h in range(nh):
            Wh = W[h]
            if p is not None:
                Wh = Wh[:, p]
            if p2 is not None:
                Wh = Wh[:, p2]
            cols.append(Wh)
        return np.concatenate(cols, axis=1)

    S = 8192
    half = dh // 2
    xscale = (np.arange(0, dh, 2, dtype=np.float64) + 0.4 * dh) / (1.4 * dh)
    pos = np.arange(S, dtype=np.float64)
    scale = xscale[None, :] ** (pos[:, None] / 512.0)
    inv_freq = 1.0 / 10000.0 ** (np.arange(half, dtype=np.float64) / half)
    sinu = pos[:, None] * inv_freq[None, :]
    sin, cos = np.sin(sinu), np.cos(sinu)
    cq = np.concatenate([cos * scale, cos * scale], -1)
    sq = np.concatenate([-sin * scale, sin * scale], -1)
    ck = np.concatenate([cos / scale, cos / scale], -1)
    sk = np.concatenate([-sin / scale, sin / scale], -1)

    def tbl(a):
        def fn(c):
            sl = slice((c % 4) * T, (c % 4) * T + T)
            t = np.ascontiguousarray(a[sl].T.astype(np.float32))
            return np.ascontiguousarray(np.tile(t, (nh, 1)))
        return fn

    cfg = dict(name="a0", nheads=nh, dh=dh,
               wq=packheads(WQ, perm), wk=packheads(WK, perm),
               wv=packheads(WV),
               wqr=packheads(WQ, perm, rotp), wkr=packheads(WK, perm, rotp),
               qm=tbl(cq), qm2=tbl(sq), km=tbl(ck), km2=tbl(sk),
               wg=host["L0_attn_W_G"].astype(np.float64),
               gammas=gammas, post="l0")
    if dbg in ("q", "k", "v", "xn"):
        cfg["dump"] = dbg
    yT = emit_retention(ctx, sb, resid_dram, core, host, cfg)
    if dbg in ("q", "k", "v", "xn"):
        src_d = ctx.scratch["a0_" + ("q" if dbg == "xn" else dbg) + "T"]
        for c in range(NCH):
            t1 = ctx.dma("sy", sb["A0"][:], src_d[c * 128:(c + 1) * 128, :])
            ctx.dma("gp", resid_dram[c * 128:(c + 1) * 128, :], sb["A0"][:],
                    waits=[t1])
            ctx.barrier()
        return
    if dbg in ("y", "sg"):
        src_d = yT if dbg == "y" else ctx.scratch["a0_sgT"]
        for c in range(NCH):
            t1 = ctx.dma("sy", sb["A0"][:], src_d[c * 128:(c + 1) * 128, :])
            ctx.dma("gp", resid_dram[c * 128:(c + 1) * 128, :], sb["A0"][:],
                    waits=[t1])
            ctx.barrier()
        return
    emit_l0_post(ctx, sb, resid_dram, yT, ctx.scratch["a0_sgT"], host)


def emit_softmax_attn(ctx, sb, resid_dram, core, host, cfg):
    name = cfg["name"]
    Hh, dh = 16, 128
    kvH = cfg["kvH"]
    KD = kvH * dh
    kvsl = cfg.get("kvsl", kvH)
    X = sb["X"]
    A3, A4 = sb["A3"], sb["A4"]
    ones = sb["ones128"]
    ident = sb["ident"]
    sc = 1.0 / np.sqrt(128.0)
    qq = core % 4

    qT = ctx.dram(name + "_qT", [D, T])
    kT = ctx.dram(name + "_kT", [KD, T])
    vT = ctx.dram(name + "_vT", [KD, T])
    vt_in = ctx.dram(name + "_vtin", [T, KD])
    kTg = ctx.dram(name + "_kTg", [4 * KD, T])
    vtg = ctx.dram(name + "_vtg", [4 * T, KD])
    yT = ctx.dram(name + "_yT", [D, T])

    half = 64
    inv_freq = 1.0 / cfg["theta"] ** (np.arange(half, dtype=np.float64) / half)
    ang = np.outer(np.arange(4 * T, dtype=np.float64), inv_freq)
    cosf = np.concatenate([np.cos(ang), np.cos(ang)], -1)
    shat = np.concatenate([-np.sin(ang), np.sin(ang)], -1)

    def tbl(a, scale, nh):
        def fn(c):
            slq = slice((c % 4) * T, (c % 4) * T + T)
            t = np.ascontiguousarray((a[slq] * scale).T.astype(np.float32))
            return np.ascontiguousarray(np.tile(t, (nh, 1)))
        return fn
    qm = ctx.input(name + "qm", tbl(cosf, sc, Hh), [D, T])
    qm2 = ctx.input(name + "qm2", tbl(shat, sc, Hh), [D, T])
    km = ctx.input(name + "km", tbl(cosf, 1.0, kvH), [KD, T])
    km2 = ctx.input(name + "km2", tbl(shat, 1.0, kvH), [KD, T])
    rotp = np.concatenate([np.arange(64, 128), np.arange(0, 64)])

    def rot_cols(W):
        return np.ascontiguousarray(
            W.reshape(W.shape[0], -1, 128)[:, :, rotp].reshape(W.shape))
    wq = ctx.input(name + "wq", lambda c: prep_w(cfg["wq"]), [16, 128, 16, 128])
    wqr = ctx.input(name + "wqr", lambda c: prep_w(rot_cols(cfg["wq"])),
                    [16, 128, 16, 128])
    wk = ctx.input(name + "wk", lambda c: prep_w(cfg["wk"]),
                   [kvsl, 128, 16, 128])
    wkr = ctx.input(name + "wkr", lambda c: prep_w(rot_cols(cfg["wk"])),
                    [kvsl, 128, 16, 128])
    wv = ctx.input(name + "wv", lambda c: prep_w(cfg["wv"]),
                   [kvsl, 128, 16, 128])
    if cfg.get("bq") is not None:
        def rowfn(b, rot):
            def fn(c):
                a = b.reshape(-1, 128)
                if rot:
                    a = a[:, rotp]
                return np.ascontiguousarray(a.reshape(1, -1).astype(np.float32))
            return fn
        brq = ctx.input(name + "brq", rowfn(cfg["bq"], False), [1, D])
        brqr = ctx.input(name + "brqr", rowfn(cfg["bq"], True), [1, D])
        brk = ctx.input(name + "brk", rowfn(cfg["bk"], False), [1, KD])
        brkr = ctx.input(name + "brkr", rowfn(cfg["bk"], True), [1, KD])
        bvp = ctx.input(name + "bvp", lambda c: np.ascontiguousarray(
            cfg["bv"].reshape(kvsl, 128).T.astype(np.float32)), [128, kvsl])
        bv_sb = ctx.sbuf(name + "bvS", [128, kvsl])
        bt = [ctx.dma("sy", bv_sb[:], bvp[:, :])]
        rb_q = (brq, brqr)
        rb_k = (brk, brkr)
    else:
        rb_q, rb_k, bv_sb, bt = None, None, None, [None]

    xn_tk = emit_ln(ctx, sb, resid_dram)
    cr = emit_proj_slices(ctx, sb, wq, qT, xn_tk, extra_tbl=(wqr, qm, qm2),
                          rope_bias=rb_q)
    cr = emit_proj_slices(ctx, sb, wk, kT, xn_tk, extra_tbl=(wkr, km, km2),
                          nslices=kvsl, carry=cr, rope_bias=rb_k)
    cr = emit_proj_slices(ctx, sb, wv, vT, xn_tk, nslices=kvsl, carry=cr,
                          bias=bv_sb, bias_tk=bt[-1])
    ctx.barrier()
    if cfg.get("dump") in ("q", "k", "v", "xn"):
        return qT, yT

    vt_t = X[0]
    prev_store = None
    for c in range(NCH):
        tks = []
        for s in range(kvH):
            bi = s % 8
            ld = ctx.dma("sy", X[1][:, s * 128:(s + 1) * 128],
                         vT[s * 128:(s + 1) * 128, c * 128:(c + 1) * 128],
                         waits=[prev_store])
            ctx.transpose(bi, X[1][:, s * 128:(s + 1) * 128], ident[:],
                          waits=[ld])
            tks.append(ctx.evac(bi, "ve", lambda eng, SS, pt, s=s:
                       eng.tensor_copy(out=vt_t[:, s * 128:(s + 1) * 128],
                                       in_=pt[:, :128]), waits=[prev_store]))
        prev_store = ctx.dma("gp", vt_in[c * 128:(c + 1) * 128, 0:KD],
                             vt_t[:, 0:KD], waits=tks)
    ctx.barrier()
    rg = DEV_RG if DEV_RG is not None else [[0, 1, 2, 3], [4, 5, 6, 7]]
    nrg = len(rg[0])
    ctx.op("gp", lambda eng, SS: eng.collective_compute(
        "AllGather", ALU.bypass, replica_groups=rg,
        ins=[kT.tensor.ap().opt()],
        outs=[kTg.tensor.ap()[0:nrg * KD, :].opt()]), inc="cc")
    ctx.op("gp", lambda eng, SS: eng.collective_compute(
        "AllGather", ALU.bypass, replica_groups=rg,
        ins=[vt_in.tensor.ap().opt()],
        outs=[vtg.tensor.ap()[0:nrg * T, :].opt()]), inc="cc")
    ctx.raw("gp", lambda eng, SS, _v=ctx.cnt["cc"]: eng.wait_ge(SS["cc"], _v))
    ctx.barrier()

    causal = cfg.get("causal", False)
    kcs = cfg.get("kcs", list(range(16 * nrg)))
    nkc = len(kcs)
    epsb, epstk = ctx.constf(EPS)
    if causal:
        qpb = ctx.input(name + "qpb", lambda c: np.ascontiguousarray(
            np.broadcast_to(((c % 4) * T + np.arange(T)).astype(np.float32),
                            (128, T))), [128, T])
        kpc = ctx.input(name + "kpc", lambda c: np.ascontiguousarray(
            (np.arange(64 * 128).reshape(64, 128).T).astype(np.float32)),
            [128, 64])
        qp_sb = A3
        tq = ctx.dma("sy", qp_sb[:], qpb[:, :])
        kp_sb = ctx.sbuf(name + "kp", [128, 64])
        tk2 = ctx.dma("sy", kp_sb[:], kpc[:, :])
        mask_tks = [tq, tk2]
    else:
        kd = cfg["keepcols"]
        keep_in = ctx.input(name + "keep", lambda c: np.ascontiguousarray(
            np.stack([(kd[kc] if kd.get(kc) is not None
                       else np.ones(128)) for kc in kcs], 1)
            .astype(np.float32)), [128, nkc])
        keep_sb = ctx.sbuf(name + "kpp", [128, nkc])
        mask_tks = [ctx.dma("sy", keep_sb[:], keep_in[:, :])]
    evict_cfg = cfg.get("evict")
    if evict_cfg:
        ev_pos, last_start_g, rbv, selcnt = evict_cfg
        sel_in = ctx.input(name + "sel", lambda c: np.ascontiguousarray(
            np.stack([(np.arange(kc * 128, (kc + 1) * 128) >= ev_pos + 1) &
                      cfg["keepfull"][kc * 128:(kc + 1) * 128]
                      for kc in kcs], 1).astype(np.float32)), [128, nkc])
        sel_sb = ctx.sbuf(name + "selS", [128, nkc])
        mask_tks.append(ctx.dma("sy", sel_sb[:], sel_in[:, :]))
        u_in = ctx.input(name + "u", lambda c: np.ascontiguousarray(
            U_CONST[c // 4:c // 4 + 1, :].astype(np.float32)), [1, 16])
        u_sb = ctx.sbuf(name + "usb", [1, 16])
        mask_tks.append(ctx.dma("sy", u_sb[:], u_in[:, :]))
        il_in = ctx.input(name + "il", lambda c: np.full(
            (1, 1), 1.0 if c % 4 == 3 else 0.0, np.float32), [1, 1])
        il_sb = ctx.sbuf(name + "ilS", [1, 1])
        mask_tks.append(ctx.dma("sy", il_sb[:], il_in[:, :]))
        qlr_in = ctx.input(name + "qlr", lambda c: np.ascontiguousarray(
            (((c % 4) * T + np.arange(T)) >= last_start_g)[None, T - 512:]
            .astype(np.float32)), [1, 512])
        qlr_sb = ctx.sbuf(name + "qlrS", [1, 512])
        mask_tks.append(ctx.dma("sy", qlr_sb[:], qlr_in[:, :]))
        ekc_i = kcs.index(ev_pos // 128)
        ekk = ev_pos % 128
        pe_sb = ctx.sbuf(name + "pe", [128, 1])
        ve_sb = ctx.sbuf(name + "ve", [1, 128])
        co_sb = ctx.sbuf(name + "co", [1, 512])
    lr_sb = ctx.sbuf("smx_lr", [1, 512])
    rr_sb = ctx.sbuf("smx_rr", [1, 512])
    ss_sb = ctx.sbuf("smx_ss", [1, 512])
    sca = ctx.sbuf("smx_sca", [1, 8])

    prev_head = None
    GQ = Hh // kvH
    for h in range(Hh):
        kv = h // GQ
        lq = ctx.dma("sy", X[0][:], qT[h * 128:(h + 1) * 128, :],
                     waits=[prev_head])
        lk = [ctx.dma("sy", X[4 + r][:],
                      kTg[r * KD + kv * 128: r * KD + (kv + 1) * 128, :],
                      waits=[prev_head]) for r in range(nrg)]
        lv = [ctx.dma("sy", X[8 + i // 16][:, (i % 16) * 128:
                                           (i % 16) * 128 + 128],
                      vtg[kc * 128:(kc + 1) * 128, kv * 128:(kv + 1) * 128],
                      waits=[prev_head]) for i, kc in enumerate(kcs)]
        for r in range(nrg):
            sq = ctx.op("sc", lambda eng, SS, r=r: eng.activation(
                out=X[1][:], in_=X[4 + r][:], func=ACT.Square),
                waits=[lk[r], prev_head])
            revs = []
            for tt in range(NTT):
                ctx.mm(6, ones[:, 0:1], X[1][:, tt * 512:(tt + 1) * 512],
                       start=True, stop=True, waits=[sq], op=1)
                revs.append(ctx.evac(6, "ve", lambda eng, SS, pt, tt=tt:
                            eng.tensor_copy(out=A4[0:1, tt * 512:(tt + 1) * 512],
                                            in_=pt[0:1, :512])))
            ctx.op("ve", lambda eng, SS, r=r: eng.tensor_reduce(
                out=sca[0:1, 0:1] if False else sca[0:1, 2:3],
                in_=A4[0:1, 0:2048], axis=AX.X, op=ALU.max), waits=revs)
            t = ctx.op("ve", lambda eng, SS, r=r: (
                eng.tensor_copy(out=sca[0:1, 3:4], in_=sca[0:1, 2:3])
                if r == 0 else
                eng.tensor_tensor(out=sca[0:1, 3:4], in0=sca[0:1, 3:4],
                                  in1=sca[0:1, 2:3], op=ALU.max)))
        t = ctx.op("sc", lambda eng, SS: eng.activation(
            out=sca[0:1, 4:5], in_=sca[0:1, 3:4], func=ACT.Ln,
            bias=epsb[0:1, 0:1]), waits=[t, epstk])
        km_tk = ctx.op("sc", lambda eng, SS: eng.activation(
            out=sca[0:1, 5:6], in_=sca[0:1, 4:5], func=ACT.Exp, scale=0.5),
            waits=[t])
        sq = ctx.op("sc", lambda eng, SS: eng.activation(
            out=X[1][:], in_=X[0][:], func=ACT.Square), waits=[lq, km_tk])
        qevs = []
        for tt in range(NTT):
            ctx.mm(6, ones[:, 0:1], X[1][:, tt * 512:(tt + 1) * 512],
                   start=True, stop=True, waits=[sq], op=1)
            qevs.append(ctx.evac(6, "ve", lambda eng, SS, pt, tt=tt:
                        eng.tensor_copy(out=A4[0:1, tt * 512:(tt + 1) * 512],
                                        in_=pt[0:1, :512])))
        t = ctx.op("sc", lambda eng, SS: eng.activation(
            out=A4[0:1, 0:2048], in_=A4[0:1, 0:2048], func=ACT.Ln,
            bias=epsb[0:1, 0:1]), waits=qevs + [epstk])
        t = ctx.op("sc", lambda eng, SS: eng.activation(
            out=A4[0:1, 0:2048], in_=A4[0:1, 0:2048], func=ACT.Exp,
            scale=0.5), waits=[t])
        negb_tk = ctx.op("ve", lambda eng, SS: eng.tensor_scalar(
            out=A4[0:1, 0:2048], in0=A4[0:1, 0:2048], scalar1=sca[0:1, 5:6],
            scalar2=-1.0, op0=ALU.mult, op1=ALU.mult), waits=[t])
        prev_pmm = [[], []]
        prev_yst = None
        for qt in range(NTT):
            qsl = slice(qt * 512, (qt + 1) * 512)
            is_ev = evict_cfg and qt == NTT - 1
            for i, kc in enumerate(kcs):
                r, kcl = kc // 16, kc % 16
                par = i % 2
                P = X[2 + par][:, 0:512]
                ctx.mm(par, X[4 + r][:, kcl * 128:(kcl + 1) * 128],
                       X[0][:, qsl], start=True, stop=False,
                       waits=[lq, lk[r]], n=512)
                ctx.mm(par, ones[0:1, 0:128], A4[0:1, qsl], start=False,
                       stop=True, waits=[negb_tk], n=512)
                pe_tk = ctx.evac(par, "sc", lambda eng, SS, pt, P=P:
                                 eng.activation(out=P, in_=pt[:, :512],
                                                func=ACT.Exp),
                                 waits=prev_pmm[par])
                if causal:
                    pe_tk = ctx.op("ve", lambda eng, SS, P=P, kc=kc,
                                   qsl=qsl: eng.scalar_tensor_tensor(
                                       out=P, in0=qp_sb[:, qsl],
                                       scalar=kp_sb[:, kc:kc + 1],
                                       op0=ALU.is_ge, in1=P, op1=ALU.mult),
                                   waits=[pe_tk] + (mask_tks if i < 2 else []))
                elif cfg["keepcols"].get(kc) is not None:
                    pe_tk = ctx.op("ve", lambda eng, SS, P=P, i=i:
                                   eng.tensor_scalar(
                                       out=P, in0=P,
                                       scalar1=keep_sb[:, i:i + 1],
                                       scalar2=None, op0=ALU.mult),
                                   waits=[pe_tk] + (mask_tks if i < 2 else []))
                vsl = X[8 + i // 16][:, (i % 16) * 128:(i % 16) * 128 + 128]
                ctx.mm(2, vsl, P, start=(i == 0),
                       stop=(i == nkc - 1 and not is_ev),
                       waits=[pe_tk, lv[i]], n=512)
                ctx.mm(3, ones[:, 0:1], P,
                       start=(i == 0), stop=(i == nkc - 1),
                       waits=[pe_tk], op=1, n=512)
                if is_ev:
                    ps_tk = ctx.op("ve", lambda eng, SS, P=P, i=i:
                                   eng.tensor_scalar(
                                       out=X[1][:, 0:512], in0=P,
                                       scalar1=sel_sb[:, i:i + 1],
                                       scalar2=None, op0=ALU.mult),
                                   waits=[pe_tk, ("mm", ctx.cnt["mm"])])
                    ctx.mm(4, ones[:, 0:1], X[1][:, 0:512],
                           start=(i == 0), stop=(i == nkc - 1),
                           waits=[ps_tk], op=1, n=512)
                    if i == ekc_i:
                        cp_tk = ctx.op("ve", lambda eng, SS, P=P:
                                       eng.tensor_copy(
                                           out=pe_sb[:], in_=P[:, 511:512]),
                                       waits=[pe_tk])
                        prev_pmm[par] = [("mm", ctx.cnt["mm"]), cp_tk]
                        continue
                prev_pmm[par] = [("mm", ctx.cnt["mm"])]
            lev = ctx.evac(3, "ve", lambda eng, SS, pt: eng.tensor_copy(
                out=lr_sb[0:1, :], in_=pt[0:1, :512]))
            rtk = ctx.op("ve", lambda eng, SS: eng.reciprocal(
                out=rr_sb[0:1, :], in_=lr_sb[0:1, :]), waits=[lev])
            if is_ev:
                sev = ctx.evac(4, "ve", lambda eng, SS, pt: eng.tensor_copy(
                    out=ss_sb[0:1, :], in_=pt[0:1, :512]))
                pdma = ctx.dma("sy", sca[0:1, 6:7], pe_sb[ekk:ekk + 1, 0:1],
                               waits=[("cpve", ctx.cnt["cpve"])])
                lvv = ctx.dma("sy", ve_sb[0:1, :],
                              vtg[ev_pos:ev_pos + 1,
                                  kv * 128:(kv + 1) * 128],
                              waits=[prev_head])
                t = ctx.op("ve", lambda eng, SS: eng.tensor_scalar(
                    out=sca[0:1, 0:1], in0=sca[0:1, 6:7],
                    scalar1=rr_sb[0:1, 511:512], scalar2=None, op0=ALU.mult),
                    waits=[pdma, rtk])
                t2 = ctx.op("ve", lambda eng, SS: eng.tensor_scalar(
                    out=sca[0:1, 1:2], in0=ss_sb[0:1, 511:512],
                    scalar1=rr_sb[0:1, 511:512], scalar2=None, op0=ALU.mult),
                    waits=[sev, rtk])
                t2 = ctx.op("ve", lambda eng, SS, selcnt=selcnt:
                    eng.tensor_scalar(
                        out=sca[0:1, 1:2], in0=sca[0:1, 1:2],
                        scalar1=1.0 / selcnt, scalar2=1e-6,
                        op0=ALU.mult, op1=ALU.add), waits=[t2])
                t2 = ctx.op("ve", lambda eng, SS: eng.reciprocal(
                    out=sca[0:1, 1:2], in_=sca[0:1, 1:2]), waits=[t2])
                t = ctx.op("ve", lambda eng, SS: eng.tensor_scalar(
                    out=sca[0:1, 0:1], in0=sca[0:1, 0:1],
                    scalar1=sca[0:1, 1:2], scalar2=None, op0=ALU.mult),
                    waits=[t, t2])
                t = ctx.op("ve", lambda eng, SS: eng.tensor_single_scalar(
                    out=sca[0:1, 0:1], in_=sca[0:1, 0:1], scalar=1.0,
                    op=ALU.min), waits=[t])
                t = ctx.op("ve", lambda eng, SS: eng.tensor_single_scalar(
                    out=sca[0:1, 0:1], in_=sca[0:1, 0:1], scalar=0.0,
                    op=ALU.max), waits=[t])
                t = ctx.op("ve", lambda eng, SS, h=h: eng.tensor_scalar(
                    out=sca[0:1, 0:1], in0=sca[0:1, 0:1],
                    scalar1=u_sb[0:1, h:h + 1], scalar2=None, op0=ALU.is_gt),
                    waits=[t, mask_tks[-3]])
                fac = ctx.op("ve", lambda eng, SS, rbv=rbv: eng.tensor_scalar(
                    out=sca[0:1, 0:1], in0=sca[0:1, 0:1],
                    scalar1=il_sb[0:1, 0:1], scalar2=1.0 / rbv,
                    op0=ALU.mult, op1=ALU.mult),
                    waits=[t, mask_tks[-2]])
                t = ctx.op("ve", lambda eng, SS: eng.tensor_tensor(
                    out=co_sb[0:1, :], in0=ss_sb[0:1, :], in1=qlr_sb[0:1, :],
                    op=ALU.mult), waits=[sev, mask_tks[-1]])
                co_tk = ctx.op("ve", lambda eng, SS: eng.tensor_scalar(
                    out=co_sb[0:1, :], in0=co_sb[0:1, :],
                    scalar1=sca[0:1, 0:1], scalar2=None, op0=ALU.mult),
                    waits=[t, fac])
                ctx.mm(2, ve_sb[0:1, 0:128], co_sb[0:1, 0:512],
                       start=False, stop=True, waits=[co_tk, lvv], n=512)
            ctx.mm(5, ones[0:1, 0:128], rr_sb[0:1, 0:512], start=True,
                   stop=True, waits=[rtk], n=512)
            brc = ctx.evac(5, "ve", lambda eng, SS, pt: eng.tensor_copy(
                out=X[13][:, 0:512], in_=pt[:, :512]))
            oev = ctx.evac(2, "ve", lambda eng, SS, pt, qsl=qsl:
                           eng.tensor_tensor(out=X[12][:, qsl],
                                             in0=pt[:, :512],
                                             in1=X[13][:, 0:512],
                                             op=ALU.mult),
                           waits=[brc, prev_yst])
            prev_yst = ctx.dma("gp", yT[h * 128:(h + 1) * 128, qsl],
                               X[12][:, qsl], waits=[oev])
        prev_head = ctx.op("ve", lambda eng, SS: eng.engine_nop(),
                           waits=[("ev", ctx.cnt["ev"]), ("mm", ctx.cnt["mm"]),
                                  prev_yst])
    ctx.barrier()
    return qT, yT


def emit_a1(ctx, sb, resid_dram, core, host, dbg=None):
    Tg = 4 * T
    sbk = int(0.1 * Tg)
    rbv = int(0.4 * Tg)
    ev_pos = Tg - rbv
    keepfull = (np.arange(Tg) < sbk) | (np.arange(Tg) >= ev_pos)
    kcs = [kc for kc in range(64) if keepfull[kc * 128:(kc + 1) * 128].any()]
    keepcols = {}
    for kc in kcs:
        blk = keepfull[kc * 128:(kc + 1) * 128]
        keepcols[kc] = None if blk.all() else blk.astype(np.float64)
    cfg = dict(name="a1", kvH=16, kvsl=16, theta=10000.0, causal=False,
               wq=host["L1_attn_Wq"].astype(np.float64).T.copy(),
               wk=host["L1_attn_Wk"].astype(np.float64).T.copy(),
               wv=host["L1_attn_Wv"].astype(np.float64).T.copy(),
               kcs=kcs, keepcols=keepcols, keepfull=keepfull,
               evict=(ev_pos, (Tg - 1) // 256 * 256, rbv,
                      int((np.arange(Tg) >= ev_pos + 1)[keepfull].sum())))
    if dbg in ("q", "k", "v", "xn"):
        cfg["dump"] = dbg
    qT, yT = emit_softmax_attn(ctx, sb, resid_dram, core, host, cfg)
    if dbg in ("q", "k", "v", "xn"):
        src_d = ctx.scratch["a1_" + ("q" if dbg == "xn" else dbg) + "T"]
        for c in range(NCH):
            t1 = ctx.dma("sy", sb["A0"][:], src_d[c * 128:(c + 1) * 128, :])
            ctx.dma("gp", resid_dram[c * 128:(c + 1) * 128, :], sb["A0"][:],
                    waits=[t1])
            ctx.barrier()
        return
    if dbg == "y":
        for c in range(NCH):
            t1 = ctx.dma("sy", sb["A0"][:], yT[c * 128:(c + 1) * 128, :])
            ctx.dma("gp", resid_dram[c * 128:(c + 1) * 128, :], sb["A0"][:],
                    waits=[t1])
            ctx.barrier()
        return
    wo = ctx.input("a1wo", lambda c: prep_w(
        host["L1_attn_Wo"].astype(np.float64).T.copy()), [16, 128, 16, 128])
    X = sb["X"]
    lds = [ctx.dma("sy", X[cix][:], yT[cix * 128:(cix + 1) * 128, :])
           for cix in range(NCH)]
    cb, _ = resid_evac_maker(ctx, sb, resid_dram, 1.0)
    emit_proj(ctx, sb, wo, D, D, X, lds, cb)
    ctx.barrier()


def emit_a3(ctx, sb, resid_dram, core, host, dbg=None):
    tf = lambda k: np.asarray(host[k]).astype(np.float64)
    cfg = dict(name="a3", kvH=4, kvsl=4, theta=1000000.0, causal=True,
               wq=tf("L3_attn_Wq"), wk=tf("L3_attn_Wk"), wv=tf("L3_attn_Wv"),
               bq=tf("L3_attn_bq"), bk=tf("L3_attn_bk"), bv=tf("L3_attn_bv"))
    if dbg in ("q", "k", "v", "xn"):
        cfg["dump"] = dbg
    qT, yT = emit_softmax_attn(ctx, sb, resid_dram, core, host, cfg)
    if dbg in ("q", "k", "v", "xn"):
        src_d = ctx.scratch["a3_" + ("q" if dbg == "xn" else dbg) + "T"]
        for c in range(NCH):
            t1 = ctx.dma("sy", sb["A0"][:], src_d[c * 128:(c + 1) * 128, :])
            ctx.dma("gp", resid_dram[c * 128:(c + 1) * 128, :], sb["A0"][:],
                    waits=[t1])
            ctx.barrier()
        return
    if dbg == "y":
        for c in range(NCH):
            t1 = ctx.dma("sy", sb["A0"][:], yT[c * 128:(c + 1) * 128, :])
            ctx.dma("gp", resid_dram[c * 128:(c + 1) * 128, :], sb["A0"][:],
                    waits=[t1])
            ctx.barrier()
        return
    wo = ctx.input("a3wo", lambda c: prep_w(
        host["L3_attn_Wo"].astype(np.float64)), [16, 128, 16, 128])
    X = sb["X"]
    lds = [ctx.dma("sy", X[cix][:], yT[cix * 128:(cix + 1) * 128, :])
           for cix in range(NCH)]
    cb, _ = resid_evac_maker(ctx, sb, resid_dram, 1.0)
    emit_proj(ctx, sb, wo, D, D, X, lds, cb)
    ctx.barrier()


def emit_resid_add(ctx, sb, resid_dram, src_dram, scale=1.0):
    R = [sb["A2"], sb["A3"]]
    Y = [sb["A0"], sb["A1"]]
    prev = {}
    for c in range(NCH):
        r, y = R[c % 2], Y[c % 2]
        w = [prev.get(c - 2)]
        l1 = ctx.dma("sy", r[:], resid_dram[c * 128:(c + 1) * 128, :], waits=w)
        l2 = ctx.dma("sy", y[:], src_dram[c * 128:(c + 1) * 128, :], waits=w)
        t = ctx.op("ve", lambda eng, SS, r=r, y=y: eng.scalar_tensor_tensor(
            out=r[:], in0=y[:], scalar=float(scale), op0=ALU.mult,
            in1=r[:], op1=ALU.add), waits=[l1, l2])
        prev[c] = ctx.dma("gp", resid_dram[c * 128:(c + 1) * 128, :], r[:], waits=[t])
    return prev[NCH - 1]


def emit_a2(ctx, sb, resid_dram, core, host, dbg=None):
    betas = np.linspace(np.log2(1 / 32.0), np.log2(1 / 512.0), 8)
    gammas = 1.0 - np.exp2(betas)
    cfg = dict(name="a2", nheads=8, dh=256,
               wq=host["L2_attn_Wq"].astype(np.float64),
               wk=host["L2_attn_Wk"].astype(np.float64),
               wv=host["L2_attn_Wv"].astype(np.float64),
               bq=host["L2_attn_bq"].astype(np.float64),
               bk=host["L2_attn_bk"].astype(np.float64),
               bv=host["L2_attn_bv"].astype(np.float64),
               gammas=gammas)
    if dbg in ("q", "k", "v", "xn"):
        cfg["dump"] = dbg
    yT = emit_retention(ctx, sb, resid_dram, core, host, cfg)
    if dbg in ("q", "k", "v", "xn"):
        src_d = ctx.scratch["a2_" + ("q" if dbg == "xn" else dbg) + "T"]
        for c in range(NCH):
            t1 = ctx.dma("sy", sb["A0"][:], src_d[c * 128:(c + 1) * 128, :])
            ctx.dma("gp", resid_dram[c * 128:(c + 1) * 128, :], sb["A0"][:], waits=[t1])
            ctx.barrier()
        return
    if dbg == "y":
        for c in range(NCH):
            t1 = ctx.dma("sy", sb["A0"][:], yT[c * 128:(c + 1) * 128, :])
            ctx.dma("gp", resid_dram[c * 128:(c + 1) * 128, :], sb["A0"][:],
                    waits=[t1])
            ctx.barrier()
        return
    emit_resid_add(ctx, sb, resid_dram, yT, 1.0)


SUBLAYERS = ["f0", "a0", "f0", "f1", "a1", "f1", "f2", "a2", "f2",
             "f3", "a3", "f3"]
DEV_RG = None


def build(host, sublayers):
    nc = bass.Bass("TRN2", target_bir_lowering=False, detect_race_conditions=False)
    stack = ExitStack()
    ctx = Ctx(nc, stack)
    with stack:
        sb = arenas(ctx)
        const_tk = load_const(ctx, sb)
        x0 = ctx.input("x0T", None, [D, T])
        resid = ctx.dram("resid", [D, T])
        outx = ctx.output("outT", [D, T])
        t = ctx.dma("gp", resid[:, :], x0[:, :])
        ctx.barrier()
        for sl in sublayers:
            if sl == "f2":
                emit_l2_ffn(ctx, sb, resid, 0, host)
            elif sl == "f2xn":
                emit_l2_ffn(ctx, sb, resid, 0, host, dbg="xn")
            elif sl == "f2thr":
                emit_l2_ffn(ctx, sb, resid, 0, host, dbg="thr")
            elif sl == "f2nq":
                emit_l2_ffn(ctx, sb, resid, 0, host, dbg="nq")
            elif sl == "f2slow":
                emit_l2_ffn(ctx, sb, resid, 0, host, dbg="slow")
            elif sl == "f1":
                emit_l1_ffn(ctx, sb, resid, 0, host)
            elif sl.startswith("f1_"):
                emit_l1_ffn(ctx, sb, resid, 0, host, dbg=sl[3:])
            elif sl == "a2":
                emit_a2(ctx, sb, resid, 0, host)
            elif sl == "a2y":
                emit_a2(ctx, sb, resid, 0, host, dbg="y")
            elif sl.startswith("a2_"):
                emit_a2(ctx, sb, resid, 0, host, dbg=sl[3:])
            elif sl == "a0":
                emit_a0(ctx, sb, resid, 0, host)
            elif sl.startswith("a0_"):
                emit_a0(ctx, sb, resid, 0, host, dbg=sl[3:])
            elif sl == "f0":
                emit_l0_ffn(ctx, sb, resid, 0, host)
            elif sl.startswith("f0_"):
                emit_l0_ffn(ctx, sb, resid, 0, host, dbg=sl[3:])
            elif sl == "f3":
                emit_l3_ffn(ctx, sb, resid, 0, host)
            elif sl.startswith("f3_"):
                emit_l3_ffn(ctx, sb, resid, 0, host, dbg=sl[3:])
            elif sl == "a1":
                emit_a1(ctx, sb, resid, 0, host)
            elif sl.startswith("a1_"):
                emit_a1(ctx, sb, resid, 0, host, dbg=sl[3:])
            elif sl == "a3":
                emit_a3(ctx, sb, resid, 0, host)
            elif sl.startswith("a3_"):
                emit_a3(ctx, sb, resid, 0, host, dbg=sl[3:])
            else:
                raise NotImplementedError(sl)
            ctx.barrier()
        t = ctx.dma("gp", outx[:, :], resid[:, :])
        snaps = [(s, ctx.cnt[s]) for s in list(ctx.cnt) if s.startswith("dm")
                 and ctx.cnt[s] > 0]
        def _final(eng, SS, _sn=snaps):
            for (s, v) in _sn:
                eng.wait_ge(SS[s], v)
        ctx.raw("gp", _final)
        ctx.finalize()
    return ctx


def kernel(**inputs):
    host = {k: np.asarray(v) for k, v in inputs.items()}
    x = host["L0_attn_X"]
    ctx = build(host, SUBLAYERS)
    in_maps = []
    for c in range(8):
        b, q = c // 4, c % 4
        m = {}
        for name, fn in ctx.input_arrays.items():
            if fn is not None:
                m[name] = np.ascontiguousarray(fn(c))
        m["x0T"] = np.ascontiguousarray(x[b, q * T:(q + 1) * T, :].T)
        in_maps.append(m)
    res = run_bass_kernel_spmd(ctx.nc, in_maps, core_ids=list(range(8)))
    out = np.empty((2, 8192, 2048), dtype=np.float32)
    for c in range(8):
        b, q = c // 4, c % 4
        out[b, q * T:(q + 1) * T, :] = res.results[c]["outT"].T
    return out
